# version 01
# baseline (speedup 1.0000x reference)
import numpy as np
import ml_dtypes
from contextlib import ExitStack
import concourse.bass as bass
import concourse.mybir as mybir
from concourse.bass_utils import run_bass_kernel_spmd

F32 = mybir.dt.float32
BF16 = mybir.dt.bfloat16
F16 = mybir.dt.float16
I32 = mybir.dt.int32
AF = mybir.ActivationFunctionType
ALU = mybir.AluOpType
AX = mybir.AxisListType
NPBF16 = ml_dtypes.bfloat16

NCORES = 8
D = 2048
B = 2
T = 4096
TOK = B * T
TPC = TOK // NCORES
H = 16
DH = 128

IDENT = np.eye(128, dtype=NPBF16)
IDENT32 = np.eye(128, dtype=np.float32)



def build_A():
    nc = bass.Bass("TRN2", target_bir_lowering=False)
    x = nc.dram_tensor("x", [TPC, D], F32, kind="ExternalInput")
    gate_w = nc.dram_tensor("gate_w", [32, D], F32, kind="ExternalInput")
    Wg1 = nc.dram_tensor("Wg1", [D, 128], F32, kind="ExternalInput")
    gup = nc.dram_tensor("gup", [1408, 256], I32, kind="ExternalInput")
    gusT = nc.dram_tensor("gusT", [1408, 16], F16, kind="ExternalInput")
    dnp = nc.dram_tensor("dnp", [256, 704], I32, kind="ExternalInput")
    dnsT = nc.dram_tensor("dnsT", [256, 44], F16, kind="ExternalInput")

    xnT = nc.dram_tensor("xnT", [D, TPC], BF16, kind="ExternalOutput")
    gl_o = nc.dram_tensor("gl", [128, TPC], BF16, kind="ExternalOutput")
    we_o = nc.dram_tensor("we", [TPC, 32], F32, kind="ExternalOutput")
    wguT = nc.dram_tensor("wguT", [D, 1408], BF16, kind="ExternalOutput")
    wdnT = nc.dram_tensor("wdnT", [5632, 256], BF16, kind="ExternalOutput")
    dbg = nc.dram_tensor("dbg", [128, 8 * 2048], BF16, kind="ExternalOutput")
    dbg2 = nc.dram_tensor("dbg2", [128, 128], F32, kind="ExternalOutput")
    dbg2_sb = nc.alloc_sbuf_tensor("dbg2_sb", [128, 128], F32)

    gu_blob = nc.dram_tensor("gu_blob", [1408, D], BF16, kind="Internal")
    dn_blob = nc.dram_tensor("dn_blob", [256, 5632], BF16, kind="Internal")
    ident_d = nc.inline_tensor(IDENT, name="ident")

    NT = TPC // 128

    xsb = [nc.alloc_sbuf_tensor(f"xsb{i}", [128, D], F32) for i in range(2)]
    sq = nc.alloc_sbuf_tensor("sq", [128, D], F32)
    xnb = nc.alloc_sbuf_tensor("xnb", [128, NT * D], BF16)
    st_ = nc.alloc_sbuf_tensor("st", [128, 8], F32)
    id_sb = nc.alloc_sbuf_tensor("id_sb", [128, 128], BF16)
    id32_sb = nc.alloc_sbuf_tensor("id32_sb", [128, 128], F32)
    xnT_sb = nc.alloc_sbuf_tensor("xnT_sb", [128, 16 * TPC], BF16)
    gw_sb = nc.alloc_sbuf_tensor("gw_sb", [32, D], BF16)
    gwT = nc.alloc_sbuf_tensor("gwT", [128, 512], BF16)
    wg1_sb = nc.alloc_sbuf_tensor("wg1_sb", [128, D], BF16)
    gl_sb = nc.alloc_sbuf_tensor("gl_sb", [128, TPC], BF16)
    P_sb = nc.alloc_sbuf_tensor("P_sb", [128, 32], F32)
    m8 = nc.alloc_sbuf_tensor("m8", [128, 8], F32)
    wm = nc.alloc_sbuf_tensor("wm", [128, 32], F32)
    msk = nc.alloc_sbuf_tensor("msk", [128, 32], F32)
    rt = nc.alloc_sbuf_tensor("rt", [128, 4], F32)
    we_sb = nc.alloc_sbuf_tensor("we_sb", [128, 32 * NT], F32)
    pk = [nc.alloc_sbuf_tensor(f"pk{i}", [128, 704], I32) for i in range(2)]
    nibp = nc.alloc_sbuf_tensor("nibp", [128, 5632], I32)
    fpl = nc.alloc_sbuf_tensor("fpl", [128, 5632], BF16)
    s01 = nc.alloc_sbuf_tensor("s01", [128, 5632], BF16)
    mm_ = nc.alloc_sbuf_tensor("mm_", [128, 5632], BF16)
    aa = nc.alloc_sbuf_tensor("aa", [128, 5632], BF16)
    bb = nc.alloc_sbuf_tensor("bb", [128, 5632], BF16)
    wrow = nc.alloc_sbuf_tensor("wrow", [128, 5632], BF16)
    scf = nc.alloc_sbuf_tensor("scf", [128, 44], F32)
    cst = nc.alloc_sbuf_tensor("cst", [128, 2], F32)
    trst = [nc.alloc_sbuf_tensor(f"trst{i}", [128, 1408], BF16) for i in range(2)]

    ps_tr = [nc.alloc_psum_tensor(f"ps_tr{i}", [128, 128], BF16) for i in range(2)]
    ps_lg = nc.alloc_psum_tensor("ps_lg", [128, 32], F32)
    ps_gl = [nc.alloc_psum_tensor(f"ps_gl{i}", [128, 512], F32) for i in range(2)]

    jobs = [("gu", r, 2048, 16) for r in range(11)] + [("dn", r, 5632, 44) for r in range(2)]

    VE_LN = 2
    SE_LN = 2
    EVAC0 = VE_LN * NT
    EVAC1 = EVAC0 + 16
    RT0 = EVAC1 + 128 + 1
    RT1 = RT0 + 2 * NT
    TE_LG0 = 16 + 128
    TE_GL0 = TE_LG0 + NT

    with (
        nc.semaphore("ld") as ld,
        nc.semaphore("gld") as gld,
        nc.semaphore("pl") as pl,
        nc.semaphore("ve") as ve,
        nc.semaphore("se") as se,
        nc.semaphore("te") as te,
        nc.semaphore("dq") as dq,
        nc.semaphore("bl") as bl,
        nc.semaphore("stq") as stq,
        nc.Block() as block,
    ):
        @block.sync
        def _(sync):
            sync.dma_start(out=id_sb[:, :], in_=ident_d[:, :]).then_inc(ld, 16)
            for t in range(NT):
                if t >= 2:
                    sync.wait_ge(ve, VE_LN * (t - 2) + 2)
                sync.dma_start(out=xsb[t % 2][:, :], in_=x[t * 128:(t + 1) * 128, :]).then_inc(ld, 16)
            sync.wait_ge(ve, RT0)
            for kk in range(16):
                sync.dma_start(out=xnT[kk * 128:(kk + 1) * 128, :],
                               in_=xnT_sb[:, kk * TPC:(kk + 1) * TPC]).then_inc(stq, 16)
            sync.wait_ge(ve, RT1)
            sync.dma_start(out=we_o.ap().rearrange("(t p) e -> p t e", p=128),
                           in_=we_sb[:, :].rearrange("p (t e) -> p t e", e=32)).then_inc(stq, 16)
            sync.wait_ge(ve, RT1 + 2)
            sync.dma_start(out=gl_o[:, :], in_=gl_sb[:, :]).then_inc(stq, 16)
            ST0 = 16 * 18
            LD0 = 16 * (1 + NT)
            sync.wait_ge(bl, 16 * 11)
            for kk in range(16):
                if kk >= 2:
                    sync.wait_ge(stq, ST0 + 16 * (kk - 1))
                sync.dma_start(out=trst[kk % 2][:, :1408],
                               in_=gu_blob[:, kk * 128:(kk + 1) * 128], transpose=True).then_inc(ld, 16)
                sync.wait_ge(ld, LD0 + 16 * (kk + 1))
                sync.dma_start(out=wguT[kk * 128:(kk + 1) * 128, :],
                               in_=trst[kk % 2][:, :1408]).then_inc(stq, 16)
            ST1 = ST0 + 16 * 16
            LD1 = LD0 + 16 * 16
            sync.wait_ge(bl, 16 * 13)
            for kk in range(44):
                if kk >= 2:
                    sync.wait_ge(stq, ST1 + 16 * (kk - 1))
                sync.dma_start(out=trst[kk % 2][:, :256],
                               in_=dn_blob[:, kk * 128:(kk + 1) * 128], transpose=True).then_inc(ld, 16)
                sync.wait_ge(ld, LD1 + 16 * (kk + 1))
                sync.dma_start(out=wdnT[kk * 128:(kk + 1) * 128, :],
                               in_=trst[kk % 2][:, :256]).then_inc(stq, 16)
            sync.dma_start(out=dbg[:, :], in_=xnb[:, :]).then_inc(stq, 16)
            sync.wait_ge(ve, RT1)
            sync.dma_start(out=dbg2[:, :], in_=dbg2_sb[:, :]).then_inc(stq, 16)
            sync.wait_ge(stq, 16 * (18 + 16 + 44 + 2))

        @block.gpsimd
        def _(gp):
            gp.dma_start(out=gw_sb[:, :], in_=gate_w[:, :]).then_inc(gld, 16)
            gp.dma_start(out=wg1_sb[:, :].rearrange("p (k j) -> p k j", j=128),
                         in_=Wg1.ap().rearrange("(k p) j -> p k j", p=128)).then_inc(gld, 16)
            nd = 0
            for ji, (kind, r, W, NG) in enumerate(jobs):
                src_p = gup[r * 128:(r + 1) * 128, :] if kind == "gu" else dnp[r * 128:(r + 1) * 128, :]
                src_s = gusT[r * 128:(r + 1) * 128, :] if kind == "gu" else dnsT[r * 128:(r + 1) * 128, :]
                if ji >= 2:
                    gp.wait_ge(dq, ji - 1)
                gp.dma_start(out=pk[ji % 2][:, :W // 8], in_=src_p).then_inc(gld, 16)
                if ji >= 1:
                    gp.wait_ge(dq, ji)
                gp.dma_start(out=scf[:, :NG], in_=src_s).then_inc(gld, 16)
                nd += 2

        @block.scalar
        def _(sc):
            for t in range(NT):
                sc.wait_ge(ld, 16 * (t + 2))
                if t >= 2:
                    sc.wait_ge(ve, VE_LN * (t - 2) + 2)
                sc.activation(out=sq[:, :], in_=xsb[t % 2][:, :], func=AF.Square).then_inc(se, 1)
                sc.wait_ge(ve, VE_LN * t + 1)
                sc.activation(out=st_[:, 6:7], in_=st_[:, 5:6], func=AF.Ln, bias=cst[:, 0:1])
                sc.activation(out=st_[:, 7:8], in_=st_[:, 6:7], func=AF.Exp,
                              scale=-0.5)
                sc.activation(out=cst[:, 1:2], in_=cst[:, 0:1], func=AF.Copy).then_inc(se, 1)
            for t in range(NT):
                sc.wait_ge(te, TE_LG0 + t + 1)
                sc.wait_ge(ve, RT0 + 2 * t + 1)
                sc.activation(out=P_sb[:, :], in_=ps_lg[:, :], func=AF.Exp,
                              bias=rt[:, 0:1], scale=1.0)
                sc.activation(out=cst[:, 1:2], in_=cst[:, 0:1], func=AF.Copy).then_inc(se, 1)
            for ji, (kind, r, W, NG) in enumerate(jobs):
                sc.wait_ge(dq, ji + 1)
                dst = gu_blob[r * 128:(r + 1) * 128, :] if kind == "gu" else dn_blob[r * 128:(r + 1) * 128, :]
                sc.dma_start(out=dst, in_=wrow[:, :W]).then_inc(bl, 16)

        @block.vector
        def _(vec):
            vec.memset(cst[:, 0:1], 1e-5)
            for t in range(NT):
                vec.wait_ge(ld, 16 + 16 * (t + 1))
                vec.reduce_sum(out=st_[:, 0:1], in_=xsb[t % 2][:, :], axis=AX.X)
                vec.wait_ge(se, SE_LN * t + 1)
                vec.reduce_sum(out=st_[:, 2:3], in_=sq[:, :], axis=AX.X)
                vec.tensor_scalar_mul(out=st_[:, 1:2], in0=st_[:, 0:1], scalar1=1.0 / D)
                vec.tensor_scalar_mul(out=st_[:, 3:4], in0=st_[:, 2:3], scalar1=1.0 / D)
                vec.tensor_mul(out=st_[:, 4:5], in0=st_[:, 1:2], in1=st_[:, 1:2])
                vec.tensor_copy(out=cst[:, 1:2], in_=cst[:, 0:1])
                vec.tensor_sub(out=st_[:, 5:6], in0=st_[:, 3:4], in1=st_[:, 4:5])
                vec.tensor_copy(out=cst[:, 1:2], in_=cst[:, 0:1]).then_inc(ve, 1)
                vec.wait_ge(se, SE_LN * t + 2)
                vec.tensor_scalar(out=xnb[:, t * D:(t + 1) * D], in0=xsb[t % 2][:, :],
                                  scalar1=st_[:, 1:2], scalar2=st_[:, 7:8],
                                  op0=ALU.subtract, op1=ALU.mult).then_inc(ve, 1)
            for kk in range(16):
                vec.wait_ge(te, kk + 1)
                vec.tensor_copy(out=gwT[:, kk * 32:(kk + 1) * 32], in_=ps_tr[kk % 2][:, 0:32]).then_inc(ve, 1)
            for i in range(NT * 16):
                t, kk = divmod(i, 16)
                vec.wait_ge(te, 16 + i + 1)
                vec.tensor_copy(out=xnT_sb[:, kk * TPC + t * 128: kk * TPC + (t + 1) * 128],
                                in_=ps_tr[i % 2][:, :]).then_inc(ve, 1)
            vec.tensor_copy(out=cst[:, 1:2], in_=cst[:, 0:1]).then_inc(ve, 1)
            for t in range(NT):
                vec.wait_ge(te, TE_LG0 + t + 1)
                vec.reduce_max(out=rt[:, 0:1], in_=ps_lg[:, :], axis=AX.X, negate=True)
                vec.tensor_copy(out=cst[:, 1:2], in_=cst[:, 0:1]).then_inc(ve, 1)
                vec.wait_ge(se, SE_LN * NT + t + 1)
                vec.max(out=m8[:, :], in_=P_sb[:, :])
                vec.reduce_sum(out=rt[:, 1:2], in_=P_sb[:, :], axis=AX.X)
                vec.tensor_scalar(out=msk[:, :], in0=P_sb[:, :], scalar1=m8[:, 3:4],
                                  scalar2=None, op0=ALU.is_ge)
                vec.tensor_copy(out=cst[:, 1:2], in_=cst[:, 0:1])
                vec.tensor_mul(out=wm[:, :], in0=msk[:, :], in1=P_sb[:, :])
                vec.tensor_copy(out=cst[:, 1:2], in_=cst[:, 0:1])
                vec.reduce_sum(out=rt[:, 3:4], in_=wm[:, :], axis=AX.X)
                vec.tensor_copy(out=cst[:, 1:2], in_=cst[:, 0:1])
                vec.scalar_tensor_tensor(out=rt[:, 2:3], in0=rt[:, 1:2], scalar=1e-20,
                                         in1=rt[:, 3:4], op0=ALU.mult, op1=ALU.add)
                vec.tensor_copy(out=cst[:, 1:2], in_=cst[:, 0:1])
                vec.reciprocal(out=rt[:, 2:3], in_=rt[:, 2:3])
                vec.tensor_copy(out=cst[:, 1:2], in_=cst[:, 0:1])
                vec.tensor_mul(out=we_sb[:, 32 * t:32 * (t + 1)], in0=wm[:, :],
                               in1=rt[:, 2:3].broadcast_to([128, 32]))
                if t == NT - 1:
                    vec.tensor_copy(out=dbg2_sb[:, 0:32], in_=P_sb[:, :])
                    vec.tensor_copy(out=dbg2_sb[:, 32:40], in_=m8[:, :])
                    vec.tensor_copy(out=dbg2_sb[:, 40:44], in_=rt[:, :])
                    vec.tensor_copy(out=dbg2_sb[:, 44:76], in_=wm[:, :])
                    vec.tensor_copy(out=dbg2_sb[:, 76:108], in_=ps_lg[:, :])
                vec.tensor_copy(out=cst[:, 1:2], in_=cst[:, 0:1]).then_inc(ve, 1)
            for half in range(2):
                vec.wait_ge(te, TE_GL0 + half + 1)
                vec.tensor_copy(out=gl_sb[:, half * 512:(half + 1) * 512],
                                in_=ps_gl[half][:, :]).then_inc(ve, 1)
            for ji, (kind, r, W, NG) in enumerate(jobs):
                vec.wait_ge(gld, 32 + 16 * (2 * ji + 2))
                if ji >= 1:
                    vec.wait_ge(bl, 16 * ji)
                for sh in range(8):
                    vec.tensor_scalar(out=nibp[:, sh * (W // 8):(sh + 1) * (W // 8)],
                                      in0=pk[ji % 2][:, :W // 8], scalar1=4 * sh, scalar2=15,
                                      op0=ALU.logical_shift_right, op1=ALU.bitwise_and)
                vec.tensor_copy(out=fpl[:, :W], in_=nibp[:, :W])
                vec.tensor_scalar(out=s01[:, :W], in0=fpl[:, :W], scalar1=8.0, scalar2=None, op0=ALU.is_ge)
                vec.scalar_tensor_tensor(out=mm_[:, :W], in0=s01[:, :W], scalar=-8.0, in1=fpl[:, :W],
                                         op0=ALU.mult, op1=ALU.add)
                vec.tensor_scalar(out=aa[:, :W], in0=mm_[:, :W], scalar1=4.0, scalar2=0.0,
                                  op0=ALU.subtract, op1=ALU.max)
                vec.tensor_scalar(out=bb[:, :W], in0=mm_[:, :W], scalar1=6.0, scalar2=0.0,
                                  op0=ALU.subtract, op1=ALU.max)
                vec.tensor_add(out=aa[:, :W], in0=aa[:, :W], in1=mm_[:, :W])
                vec.scalar_tensor_tensor(out=mm_[:, :W], in0=aa[:, :W], scalar=0.5, in1=bb[:, :W],
                                         op0=ALU.mult, op1=ALU.add)
                vec.tensor_scalar(out=s01[:, :W], in0=s01[:, :W], scalar1=-2.0, scalar2=1.0,
                                  op0=ALU.mult, op1=ALU.add)
                vec.tensor_mul(out=mm_[:, :W], in0=mm_[:, :W], in1=s01[:, :W])
                gathered = mm_[:, :W].rearrange("p (s j) -> p j s", s=8)
                for g in range(NG):
                    ins = vec.tensor_scalar(
                        out=wrow[:, g * 128:(g + 1) * 128].rearrange("p (j s) -> p j s", s=8),
                        in0=gathered[:, 16 * g:16 * (g + 1), :],
                        scalar1=scf[:, g:g + 1], scalar2=None, op0=ALU.mult)
                    pass
                vec.tensor_copy(out=cst[:, 1:2], in_=cst[:, 0:1]).then_inc(dq, 1)

        @block.tensor
        def _(tn):
            tn.wait_ge(gld, 16)
            tn.wait_ge(ld, 16)
            for kk in range(16):
                if kk >= 2:
                    tn.wait_ge(ve, EVAC0 + kk - 1)
                tn.transpose(ps_tr[kk % 2][:, 0:32], gw_sb[0:32, kk * 128:(kk + 1) * 128],
                             id_sb[0:32, 0:32]).then_inc(te, 1)
            for i in range(NT * 16):
                t, kk = divmod(i, 16)
                tn.wait_ge(ve, VE_LN * (t + 1))
                tn.wait_ge(ve, EVAC0 + 15 + i if i < 2 else EVAC1 + i - 1)
                tn.transpose(ps_tr[i % 2][:, :], xnb[:, t * D + kk * 128: t * D + (kk + 1) * 128],
                             id_sb[:, :]).then_inc(te, 1)
            for t in range(NT):
                tn.wait_ge(ve, min(EVAC1 + 16 * (t + 1) + 1, RT0))
                if t >= 1:
                    tn.wait_ge(se, SE_LN * NT + t)
                    tn.wait_ge(ve, RT0 + 2 * (t - 1) + 1)
                for kk in range(16):
                    ins = tn.matmul(ps_lg[:, :], xnT_sb[:, kk * TPC + t * 128: kk * TPC + (t + 1) * 128],
                                    gwT[:, kk * 32:(kk + 1) * 32], start=(kk == 0), stop=(kk == 15))
                    if kk == 15:
                        ins.then_inc(te, 1)
            tn.wait_ge(gld, 32)
            tn.wait_ge(ve, RT0)
            for half in range(2):
                for kk in range(16):
                    ins = tn.matmul(ps_gl[half][:, :], wg1_sb[:, kk * 128:(kk + 1) * 128],
                                    xnT_sb[:, kk * TPC + half * 512: kk * TPC + half * 512 + 512],
                                    start=(kk == 0), stop=(kk == 15))
                    if kk == 15:
                        ins.then_inc(te, 1)

    return nc



def run_A(inputs):
    x_full = np.ascontiguousarray(np.asarray(inputs["L0_attn_x"], dtype=np.float32).reshape(TOK, D))
    gate_w = np.ascontiguousarray(np.asarray(inputs["L0_ffn_gate_w"], dtype=np.float32))
    Wg1 = np.ascontiguousarray(np.asarray(inputs["L0_attn_Wg1"], dtype=np.float32))
    gup = np.asarray(inputs["L1_ffn_gate_up_packed"], dtype=np.int32)
    gusT = np.asarray(inputs["L1_ffn_gate_up_scales"], dtype=np.float16).T
    dnp = np.asarray(inputs["L1_ffn_down_packed"], dtype=np.int32)
    dnsT = np.asarray(inputs["L1_ffn_down_scales"], dtype=np.float16).T
    in_maps = []
    for c in range(NCORES):
        in_maps.append({
            "x": x_full[c * TPC:(c + 1) * TPC],
            "gate_w": gate_w,
            "Wg1": Wg1,
            "gup": np.ascontiguousarray(gup[c * 1408:(c + 1) * 1408]),
            "gusT": np.ascontiguousarray(gusT[c * 1408:(c + 1) * 1408]),
            "dnp": np.ascontiguousarray(dnp[c * 256:(c + 1) * 256]),
            "dnsT": np.ascontiguousarray(dnsT[c * 256:(c + 1) * 256]),
        })
    nc = build_A()
    res = run_bass_kernel_spmd(nc, in_maps, core_ids=list(range(NCORES)))
    return res.results


def kernel(**inputs):
    a_out = run_A(inputs)
    b_out = run_B(inputs, a_out)
    c_out = run_C(inputs, a_out)
    d_out = run_D(inputs, a_out, b_out, c_out)
    e_out = run_E(inputs, d_out)
    f_out = run_F(inputs, d_out, e_out, a_out)
    full = np.concatenate([np.asarray(f_out[c]["out"], dtype=np.float32) for c in range(NCORES)], axis=0)
    return full.reshape(B, T, D)



class Sched:
    ENGINES = ("sync", "gpsimd", "vector", "scalar", "tensor")

    def __init__(self):
        self.ops = {e: [] for e in self.ENGINES}
        self.cnt = {}

    def val(self, sem):
        return self.cnt.get(sem, 0)

    def op(self, eng, fn, waits=(), inc=None, inc_n=1):
        if inc is not None:
            self.cnt[inc] = self.cnt.get(inc, 0) + inc_n
        self.ops[eng].append((tuple(waits), fn, inc, inc_n))
        return self.cnt.get(inc, None)

    def emit(self, nc, sem_handles):
        with nc.Block() as block:
            def body(eng_name):
                def _(eng):
                    for waits, fn, inc, inc_n in self.ops[eng_name]:
                        for s, v in waits:
                            eng.wait_ge(sem_handles[s], v)
                        ins = fn(eng)
                        if inc is not None:
                            ins.then_inc(sem_handles[inc], inc_n)
                return _
            block.sync(body("sync"))
            block.gpsimd(body("gpsimd"))
            block.vector(body("vector"))
            block.scalar(body("scalar"))
            block.tensor(body("tensor"))


def make_sems(nc, stack, names):
    import contextlib
    handles = {}
    for n in names:
        handles[n] = stack.enter_context(nc.semaphore(n))
    return handles



def build_B():
    from contextlib import ExitStack
    nc = bass.Bass("TRN2", target_bir_lowering=False)
    TT = TPC

    xnT_d = nc.dram_tensor("xnT", [D, TT], BF16, kind="ExternalInput")
    weT_d = nc.dram_tensor("weT", [32, TT], F32, kind="ExternalInput")
    Rg_d = nc.dram_tensor("Rg", [128, D], F32, kind="ExternalInput")
    Ru_d = nc.dram_tensor("Ru", [128, D], F32, kind="ExternalInput")
    Rd_d = nc.dram_tensor("Rd", [128, 1408], F32, kind="ExternalInput")
    Ug_d = nc.dram_tensor("Ug", [32, 128, 128], F32, kind="ExternalInput")
    Cg_d = nc.dram_tensor("Cg", [32, 1408, 128], F32, kind="ExternalInput")
    Uu_d = nc.dram_tensor("Uu", [32, 128, 128], F32, kind="ExternalInput")
    Cu_d = nc.dram_tensor("Cu", [32, 1408, 128], F32, kind="ExternalInput")
    Ud_d = nc.dram_tensor("Ud", [32, 128, 128], F32, kind="ExternalInput")
    Cd_d = nc.dram_tensor("Cd", [32, 2048, 128], F32, kind="ExternalInput")
    sRg_d = nc.dram_tensor("sRg", [128, D], F32, kind="ExternalInput")
    sRu_d = nc.dram_tensor("sRu", [128, D], F32, kind="ExternalInput")
    sRd_d = nc.dram_tensor("sRd", [128, 2816], F32, kind="ExternalInput")
    sUg_d = nc.dram_tensor("sUg", [128, 128], F32, kind="ExternalInput")
    sUu_d = nc.dram_tensor("sUu", [128, 128], F32, kind="ExternalInput")
    sUd_d = nc.dram_tensor("sUd", [128, 128], F32, kind="ExternalInput")
    sCg_d = nc.dram_tensor("sCg", [2816, 128], F32, kind="ExternalInput")
    sCu_d = nc.dram_tensor("sCu", [2816, 128], F32, kind="ExternalInput")
    sCd_d = nc.dram_tensor("sCd", [2048, 128], F32, kind="ExternalInput")
    y_d = nc.dram_tensor("y", [TT, D], F32, kind="ExternalOutput")

    bRg = nc.dram_tensor("bRg", [128, D], BF16, kind="Internal")
    bRu = nc.dram_tensor("bRu", [128, D], BF16, kind="Internal")
    bRd = nc.dram_tensor("bRd", [128, 1408], BF16, kind="Internal")
    bsRg = nc.dram_tensor("bsRg", [128, D], BF16, kind="Internal")
    bsRu = nc.dram_tensor("bsRu", [128, D], BF16, kind="Internal")
    bsRd = nc.dram_tensor("bsRd", [128, 2816], BF16, kind="Internal")
    bsUg = nc.dram_tensor("bsUg", [128, 128], BF16, kind="Internal")
    bsUu = nc.dram_tensor("bsUu", [128, 128], BF16, kind="Internal")
    bsUd = nc.dram_tensor("bsUd", [128, 128], BF16, kind="Internal")
    bsCg = nc.dram_tensor("bsCg", [2816, 128], BF16, kind="Internal")
    bsCu = nc.dram_tensor("bsCu", [2816, 128], BF16, kind="Internal")
    bsCd = nc.dram_tensor("bsCd", [2048, 128], BF16, kind="Internal")
    bUg = nc.dram_tensor("bUg", [32, 128, 128], BF16, kind="Internal")
    bCg = nc.dram_tensor("bCg", [32, 1408, 128], BF16, kind="Internal")
    bUu = nc.dram_tensor("bUu", [32, 128, 128], BF16, kind="Internal")
    bCu = nc.dram_tensor("bCu", [32, 1408, 128], BF16, kind="Internal")
    bUd = nc.dram_tensor("bUd", [32, 128, 128], BF16, kind="Internal")
    bCd = nc.dram_tensor("bCd", [32, 2048, 128], BF16, kind="Internal")

    xnT = nc.alloc_sbuf_tensor("xnT_sb", [128, 16 * TT], BF16)
    RgT = nc.alloc_sbuf_tensor("RgT", [128, D], BF16)
    RuT = nc.alloc_sbuf_tensor("RuT", [128, D], BF16)
    RdT = nc.alloc_sbuf_tensor("RdT", [128, 1408], BF16)
    sRgT = nc.alloc_sbuf_tensor("sRgT", [128, D], BF16)
    sRuT = nc.alloc_sbuf_tensor("sRuT", [128, D], BF16)
    sRdT = nc.alloc_sbuf_tensor("sRdT", [128, 2816], BF16)
    sUgT = nc.alloc_sbuf_tensor("sUgT", [128, 128], BF16)
    sUuT = nc.alloc_sbuf_tensor("sUuT", [128, 128], BF16)
    sUdT = nc.alloc_sbuf_tensor("sUdT", [128, 128], BF16)
    sCgT = nc.alloc_sbuf_tensor("sCgT", [128, 2816], BF16)
    sCuT = nc.alloc_sbuf_tensor("sCuT", [128, 2816], BF16)
    sCdT = nc.alloc_sbuf_tensor("sCdT", [128, 2048], BF16)
    UgB = [nc.alloc_sbuf_tensor(f"UgB{i}", [128, 128], BF16) for i in range(2)]
    UuB = [nc.alloc_sbuf_tensor(f"UuB{i}", [128, 128], BF16) for i in range(2)]
    UdB = [nc.alloc_sbuf_tensor(f"UdB{i}", [128, 128], BF16) for i in range(2)]
    CgB = [nc.alloc_sbuf_tensor(f"CgB{i}", [128, 1408], BF16) for i in range(2)]
    CuB = [nc.alloc_sbuf_tensor(f"CuB{i}", [128, 1408], BF16) for i in range(2)]
    CdS = [nc.alloc_sbuf_tensor(f"CdS{i}", [128, 2048], BF16) for i in range(8)]
    rgT = nc.alloc_sbuf_tensor("rgT", [128, TT], BF16)
    ruT = nc.alloc_sbuf_tensor("ruT", [128, TT], BF16)
    g1T = nc.alloc_sbuf_tensor("g1T", [128, TT], BF16)
    u1T = nc.alloc_sbuf_tensor("u1T", [128, TT], BF16)
    sg = nc.alloc_sbuf_tensor("sg", [128, TT], BF16)
    hc = nc.alloc_sbuf_tensor("hc", [128, TT], BF16)
    rdT = nc.alloc_sbuf_tensor("rdT", [128, TT], BF16)
    gaT = rdT
    uaT = sg
    webc = nc.alloc_sbuf_tensor("webc", [128, TT], F32)
    RD2T = nc.alloc_sbuf_tensor("RD2T", [128, 33 * TT], BF16)
    yev = [nc.alloc_sbuf_tensor(f"yev{i}", [128, 512], F32) for i in range(2)]

    pp = [nc.alloc_psum_tensor(f"pp{i}", [128, 512], F32) for i in range(4)]
    pr = [nc.alloc_psum_tensor(f"pr{i}", [128, 512], F32) for i in range(2)]
    py = [nc.alloc_psum_tensor(f"py{i}", [128, 512], F32) for i in range(2)]

    sch = Sched()
    bfree = {i: None for i in range(8)}

    def wait_bank(i):
        return [bfree[i]] if bfree[i] is not None else []

    sch.op("sync", lambda e: e.dma_start(
        out=xnT[:, :].rearrange("p (k t) -> p k t", t=TT),
        in_=xnT_d.ap().rearrange("(k p) t -> p k t", p=128)), inc="ld", inc_n=16)
    ld_xnT = sch.val("ld")

    cast_list = [(bRg, Rg_d), (bRu, Ru_d), (bRd, Rd_d), (bsRg, sRg_d), (bsRu, sRu_d),
                 (bsRd, sRd_d), (bsUg, sUg_d), (bsUu, sUu_d), (bsUd, sUd_d),
                 (bsCg, sCg_d), (bsCu, sCu_d), (bsCd, sCd_d)]
    cast_done = {}
    for blob, src in cast_list:
        sch.op("gpsimd", lambda e, b=blob, s=src: e.dma_start(out=b[:, :], in_=s[:, :]),
               inc="gld", inc_n=16)
        cast_done[blob.name] = sch.val("gld")
    ecast_done = {}
    for ei in range(32):
        for blob, src in [(bUg, Ug_d), (bUu, Uu_d), (bUd, Ud_d), (bCg, Cg_d), (bCu, Cu_d), (bCd, Cd_d)]:
            sch.op("gpsimd", lambda e, b=blob, s=src, i=ei: e.dma_start(out=b[i], in_=s[i]),
                   inc="gld", inc_n=16)
            ecast_done[(blob.name, ei)] = sch.val("gld")

    tl_done = {}
    def tload(dst, dst_cols, blob, src_cols, wait):
        sch.op("sync", lambda e, d=dst, dc=dst_cols, b=blob, sc=src_cols:
               e.dma_start(out=d[:, dc[0]:dc[1]], in_=b[:, sc[0]:sc[1]], transpose=True),
               waits=[("gld", wait)], inc="ld", inc_n=16)
        return sch.val("ld")

    for k in range(16):
        tl_done[("RgT", k)] = tload(RgT, (k * 128, (k + 1) * 128), bRg, (k * 128, (k + 1) * 128), cast_done["bRg"])
    for k in range(16):
        tl_done[("RuT", k)] = tload(RuT, (k * 128, (k + 1) * 128), bRu, (k * 128, (k + 1) * 128), cast_done["bRu"])
    for k in range(11):
        tl_done[("RdT", k)] = tload(RdT, (k * 128, (k + 1) * 128), bRd, (k * 128, (k + 1) * 128), cast_done["bRd"])
    for k in range(16):
        tl_done[("sRgT", k)] = tload(sRgT, (k * 128, (k + 1) * 128), bsRg, (k * 128, (k + 1) * 128), cast_done["bsRg"])
    for k in range(16):
        tl_done[("sRuT", k)] = tload(sRuT, (k * 128, (k + 1) * 128), bsRu, (k * 128, (k + 1) * 128), cast_done["bsRu"])
    for k in range(22):
        tl_done[("sRdT", k)] = tload(sRdT, (k * 128, (k + 1) * 128), bsRd, (k * 128, (k + 1) * 128), cast_done["bsRd"])
    sch.op("sync", lambda e: e.dma_start(out=sUgT[:, :], in_=bsUg[:, :], transpose=True),
           waits=[("gld", cast_done["bsUg"])], inc="ld", inc_n=16)
    sch.op("sync", lambda e: e.dma_start(out=sUuT[:, :], in_=bsUu[:, :], transpose=True),
           waits=[("gld", cast_done["bsUu"])], inc="ld", inc_n=16)
    sch.op("sync", lambda e: e.dma_start(out=sUdT[:, :], in_=bsUd[:, :], transpose=True),
           waits=[("gld", cast_done["bsUd"])], inc="ld", inc_n=16)
    sch.op("sync", lambda e: e.dma_start(out=sCgT[:, :], in_=bsCg[:, :], transpose=True),
           waits=[("gld", cast_done["bsCg"])], inc="ld", inc_n=16)
    sch.op("sync", lambda e: e.dma_start(out=sCuT[:, :], in_=bsCu[:, :], transpose=True),
           waits=[("gld", cast_done["bsCu"])], inc="ld", inc_n=16)
    sch.op("sync", lambda e: e.dma_start(out=sCdT[:, :], in_=bsCd[:, :], transpose=True),
           waits=[("gld", cast_done["bsCd"])], inc="ld", inc_n=16)
    ld_sprep = sch.val("ld")

    for which, dstT, srcT, keyname in [(0, rgT, RgT, "RgT"), (1, ruT, RuT, "RuT")]:
        for hf in range(2):
            bank = which * 2 + hf
            for k in range(16):
                w = [("ld", max(tl_done[(keyname, k)], ld_xnT))]
                if k == 0:
                    w += wait_bank(bank)
                sch.op("tensor", lambda e, b=bank, kk=k, h=hf, sT=srcT:
                       e.matmul(pp[b][:, :], sT[:, kk * 128:(kk + 1) * 128],
                                xnT[:, kk * TT + h * 512: kk * TT + h * 512 + 512],
                                start=(kk == 0), stop=(kk == 15)),
                       waits=w, inc="te" if k == 15 else None)
            te_v = sch.val("te")
            sch.op("vector", lambda e, b=bank, h=hf, dT=dstT:
                   e.tensor_copy(out=dT[:, h * 512:(h + 1) * 512], in_=pp[b][:, :]),
                   waits=[("te", te_v)], inc="ve")
            bfree[bank] = ("ve", sch.val("ve"))
    ve_rgru = sch.val("ve")

    exp_ld = {}
    cd_ld = {}
    tex_after = {}
    rd2_ve = {}

    def plan_exp_loads(ei):
        pw = [("tex", max(0, ei - 1))] if ei >= 2 else []
        for dst, blob, w_, key in [
            (UgB[ei % 2], bUg, 128, "bUg"), (UuB[ei % 2], bUu, 128, "bUu"),
            (UdB[ei % 2], bUd, 128, "bUd"), (CgB[ei % 2], bCg, 1408, "bCg"),
            (CuB[ei % 2], bCu, 1408, "bCu"),
        ]:
            sch.op("sync", lambda e, d=dst, b=blob, i=ei: e.dma_start(out=d[:, :], in_=b[i], transpose=True),
                   waits=[("gld", ecast_done[(key, ei)])] + pw, inc="ld", inc_n=16)
        exp_ld[ei] = sch.val("ld")
        g = ei // 8
        wcd = [("gld", ecast_done[("bCd", ei)])]
        if g >= 1:
            wcd.append(("yp", 16 * g))
        sch.op("sync", lambda e, i=ei: e.dma_start(out=CdS[ei % 8][:, :], in_=bCd[i], transpose=True),
               waits=wcd, inc="ld", inc_n=16)
        cd_ld[ei] = sch.val("ld")
        sch.op("gpsimd", lambda e, i=ei: e.dma_start(
            out=webc[:, :], in_=weT_d.ap()[i:i + 1, :].broadcast_to([128, TT])),
            waits=[("web", 17 * ei)], inc="web", inc_n=16)

    def plan_expert(ei):
        shared = (ei == 32)
        n_oc = 22 if shared else 11
        CgS = sCgT if shared else CgB[ei % 2]
        CuS = sCuT if shared else CuB[ei % 2]
        RdS = sRdT if shared else RdT
        UdS = sUdT if shared else UdB[ei % 2]
        wld = [] if shared else [("ld", exp_ld[ei])]

        if not shared:
            for bank, Uw, rsrc, dstT in [(0, UgB[ei % 2], rgT, g1T), (1, UgB[ei % 2], rgT, g1T),
                                         (2, UuB[ei % 2], ruT, u1T), (3, UuB[ei % 2], ruT, u1T)]:
                pass
            for idx, (Uw, rsrc, dstT) in enumerate([(UgB[ei % 2], rgT, g1T), (UuB[ei % 2], ruT, u1T)]):
                for hf in range(2):
                    bank = idx * 2 + hf
                    sch.op("tensor", lambda e, b=bank, U=Uw, r=rsrc, h=hf:
                           e.matmul(pp[b][:, :], U[:, :], r[:, h * 512:(h + 1) * 512],
                                    start=True, stop=True),
                           waits=wld + [("ve", ve_rgru)] + wait_bank(bank), inc="te")
                    tev = sch.val("te")
                    sch.op("vector", lambda e, b=bank, h=hf, dT=dstT:
                           e.tensor_copy(out=dT[:, h * 512:(h + 1) * 512], in_=pp[b][:, :]),
                           waits=[("te", tev)], inc="ve")
                    bfree[bank] = ("ve", sch.val("ve"))
            g1_ve = sch.val("ve")
            gsrcT, usrcT = g1T, u1T
        else:
            for idx, (RT_, UT_, dstT) in enumerate([(sRgT, sUgT, g1T), (sRuT, sUuT, u1T)]):
                for hf in range(2):
                    bank = idx * 2 + hf
                    for k in range(16):
                        w = wait_bank(bank) if k == 0 else []
                        sch.op("tensor", lambda e, b=bank, kk=k, h=hf, R=RT_:
                               e.matmul(pp[b][:, :], R[:, kk * 128:(kk + 1) * 128],
                                        xnT[:, kk * TT + h * 512: kk * TT + h * 512 + 512],
                                        start=(kk == 0), stop=(kk == 15)),
                               waits=w, inc="te" if k == 15 else None)
                    tev = sch.val("te")
                    dst_ga = gaT if idx == 0 else uaT
                    sch.op("vector", lambda e, b=bank, h=hf, dT=dst_ga:
                           e.tensor_copy(out=dT[:, h * 512:(h + 1) * 512], in_=pp[b][:, :]),
                           waits=[("te", tev)], inc="ve")
                    bfree[bank] = ("ve", sch.val("ve"))
                ga_ve = sch.val("ve")
                for hf in range(2):
                    bank = idx * 2 + hf
                    src_ga = gaT if idx == 0 else uaT
                    sch.op("tensor", lambda e, b=bank, U=UT_, h=hf, sga=src_ga:
                           e.matmul(pp[b][:, :], U[:, :], sga[:, h * 512:(h + 1) * 512],
                                    start=True, stop=True),
                           waits=[("ve", ga_ve)] + wait_bank(bank), inc="te")
                    tev = sch.val("te")
                    sch.op("vector", lambda e, b=bank, h=hf, dT=dstT:
                           e.tensor_copy(out=dT[:, h * 512:(h + 1) * 512], in_=pp[b][:, :]),
                           waits=[("te", tev)], inc="ve")
                    bfree[bank] = ("ve", sch.val("ve"))
            g1_ve = sch.val("ve")
            gsrcT, usrcT = g1T, u1T

        for hf in range(2):
            rbank = 4 + hf
            for oc in range(n_oc):
                po = oc % 2
                w_g = [("ve", g1_ve)] + wait_bank(po)
                sch.op("tensor", lambda e, b=po, o=oc, h=hf, C=CgS, gT=gsrcT:
                       e.matmul(pp[b][:, :], C[:, o * 128:(o + 1) * 128],
                                gT[:, h * 512:(h + 1) * 512], start=True, stop=True),
                       waits=w_g)
                w_u = wait_bank(2 + po)
                sch.op("tensor", lambda e, b=2 + po, o=oc, h=hf, C=CuS, uT=usrcT:
                       e.matmul(pp[b][:, :], C[:, o * 128:(o + 1) * 128],
                                uT[:, h * 512:(h + 1) * 512], start=True, stop=True),
                       waits=w_u, inc="te")
                tev = sch.val("te")
                sch.op("scalar", lambda e, b=po: e.activation(
                    out=sg[:, b * 512:(b + 1) * 512], in_=pp[b][:, :], func=AF.Silu),
                    waits=[("te", tev)], inc="se")
                sev = sch.val("se")
                bfree[po] = ("se", sev)
                sch.op("vector", lambda e, b=po: e.tensor_mul(
                    out=hc[:, b * 512:(b + 1) * 512], in0=sg[:, b * 512:(b + 1) * 512],
                    in1=pp[2 + b][:, :]),
                    waits=[("se", sev)], inc="ve")
                vev = sch.val("ve")
                bfree[2 + po] = ("ve", vev)
                sch.op("tensor", lambda e, rb=rbank, o=oc, b=po, R=RdS, n=n_oc:
                       e.matmul(pr[rb - 4][:, :], R[:, o * 128:(o + 1) * 128],
                                hc[:, b * 512:(b + 1) * 512], start=(o == 0), stop=(o == n - 1)),
                       waits=[("ve", vev)] + (wait_bank(rbank) if oc == 0 else []),
                       inc="te")
            te_rd = sch.val("te")
            sch.op("vector", lambda e, h=hf, rb=rbank: e.tensor_copy(
                out=rdT[:, h * 512:(h + 1) * 512], in_=pr[rb - 4][:, :]),
                waits=[("te", te_rd)], inc="ve")
            bfree[rbank] = ("ve", sch.val("ve"))
        rd_ve = sch.val("ve")

        for hf in range(2):
            sch.op("tensor", lambda e, b=hf, U=UdS, h=hf:
                   e.matmul(pp[b][:, :], U[:, :], rdT[:, h * 512:(h + 1) * 512],
                            start=True, stop=True),
                   waits=[("ve", rd_ve)] + wait_bank(hf), inc="te")
            tev = sch.val("te")
            if shared:
                sch.op("vector", lambda e, b=hf, h=hf, i=ei: e.tensor_copy(
                    out=RD2T[:, i * TT + h * 512: i * TT + (h + 1) * 512], in_=pp[b][:, :]),
                    waits=[("te", tev)], inc="ve")
            else:
                w_web = [("web", 17 * ei + 16)]
                sch.op("vector", lambda e, b=hf, h=hf, i=ei: e.tensor_mul(
                    out=RD2T[:, i * TT + h * 512: i * TT + (h + 1) * 512],
                    in0=webc[:, h * 512:(h + 1) * 512], in1=pp[b][:, :]),
                    waits=[("te", tev)] + w_web, inc="ve")
            bfree[hf] = ("ve", sch.val("ve"))
        if not shared:
            sch.op("vector", lambda e: e.tensor_copy(out=yev[0][:, 0:1], in_=yev[0][:, 0:1]),
                   inc="web")
        rd2_ve[ei] = sch.val("ve")
        sch.op("tensor", lambda e: e.matmul(pp[0][0:1, 0:1], xnT[0:1, 0:1], xnT[0:1, 0:1],
                                            start=True, stop=True),
               waits=wait_bank(0), inc="tex") if False else None
        sch.op("scalar", lambda e: e.activation(out=sg[:, 0:1], in_=sg[:, 0:1], func=AF.Copy),
               waits=[("ve", rd2_ve[ei])], inc="tex")
        tex_after[ei] = sch.val("tex")

    def plan_ypass(g):
        exps = list(range(g * 8, g * 8 + 8))
        if g == 3:
            exps = exps + [32]
        wrd = [("ve", max(rd2_ve[e_] for e_ in exps))]
        wcd = [("ld", max(cd_ld[e_] for e_ in exps if e_ < 32))]
        first = True
        for tt_i in range(8):
            for ocg in range(4):
                bank = 6 + (tt_i * 4 + ocg) % 2
                w = (wrd + wcd) if first else []
                w = w + wait_bank(bank)
                first = False
                for j, e_ in enumerate(exps):
                    rhs_src = sCdT if e_ == 32 else CdS[e_ % 8]
                    sch.op("tensor", lambda e, b=bank, i=e_, t=tt_i, o=ocg, r=rhs_src, jj=j, n=len(exps):
                           e.matmul(py[b - 6][:, :], RD2T[:, i * TT + t * 128:i * TT + (t + 1) * 128],
                                    r[:, o * 512:(o + 1) * 512], start=(jj == 0), stop=(jj == n - 1)),
                           waits=w if j == 0 else [], inc="te" if j == len(exps) - 1 else None)
                tev = sch.val("te")
                par = (tt_i * 4 + ocg) % 2
                wst = [("gst", sch.val("gst") - 16)] if sch.val("gst") >= 16 and (tt_i * 4 + ocg) >= 2 else []
                sch.op("vector", lambda e, b=bank, p=par: e.tensor_copy(out=yev[p][:, :], in_=py[b - 6][:, :]),
                       waits=[("te", tev)] + wst, inc="ve")
                vev = sch.val("ve")
                bfree[bank] = ("ve", vev)
                sch.op("gpsimd", lambda e, t=tt_i, o=ocg, p=par, gg=g: e.dma_start(
                    out=y_d[t * 128:(t + 1) * 128, o * 512:(o + 1) * 512], in_=yev[p][:, :],
                    accum_op=(ALU.bypass if gg == 0 else ALU.add)),
                    waits=[("ve", vev)], inc="gst", inc_n=16)
        sch.op("gpsimd", lambda e: e.dma_start(out=bsUg[0:1, 0:1], in_=sUg_d[0:1, 0:1]),
               waits=[("gst", sch.val("gst"))], inc="yp", inc_n=16)

    for ei in range(32):
        plan_exp_loads(ei)
        plan_expert(ei)
        if ei % 8 == 7 and ei != 31:
            plan_ypass(ei // 8)
    plan_expert(32)
    plan_ypass(3)

    sch.op("sync", lambda e: e.dma_start(out=bsUu[0:1, 0:1], in_=bsUg[0:1, 0:1]),
           waits=[("gst", sch.val("gst")), ("yp", sch.val("yp"))], inc="ld", inc_n=16)

    from contextlib import ExitStack
    with ExitStack() as stack:
        sems = make_sems(nc, stack, ["ld", "gld", "te", "ve", "se", "web", "tex", "gst", "yp"])
        sch.emit(nc, sems)
    return nc


def run_B(inputs, a_out):
    xnT_full = [np.asarray(a_out[c]["xnT"]) for c in range(NCORES)]
    we_full = np.concatenate([np.asarray(a_out[c]["we"]) for c in range(NCORES)], axis=0)
    weT = np.ascontiguousarray(we_full.T)
    def f(k):
        return np.ascontiguousarray(np.asarray(inputs[k], dtype=np.float32))
    in_common = {
        "Rg": f("L0_ffn_Rg"), "Ru": f("L0_ffn_Ru"), "Rd": f("L0_ffn_Rd"),
        "Ug": f("L0_ffn_Ug"), "Cg": f("L0_ffn_Cg"), "Uu": f("L0_ffn_Uu"),
        "Cu": f("L0_ffn_Cu"), "Ud": f("L0_ffn_Ud"), "Cd": f("L0_ffn_Cd"),
        "sRg": f("L0_ffn_s_Rg"), "sRu": f("L0_ffn_s_Ru"), "sRd": f("L0_ffn_s_Rd"),
        "sUg": f("L0_ffn_s_Ug"), "sUu": f("L0_ffn_s_Uu"), "sUd": f("L0_ffn_s_Ud"),
        "sCg": f("L0_ffn_s_Cg"), "sCu": f("L0_ffn_s_Cu"), "sCd": f("L0_ffn_s_Cd"),
    }
    in_maps = []
    for c in range(NCORES):
        m = dict(in_common)
        m["xnT"] = xnT_full[c]
        m["weT"] = np.ascontiguousarray(weT[:, c * TPC:(c + 1) * TPC])
        in_maps.append(m)
    nc = build_B()
    res = run_bass_kernel_spmd(nc, in_maps, core_ids=list(range(NCORES)))
    return res.results



TRIT = np.zeros((128, 16), dtype=np.float32)
for _ci in range(4):
    for _s in range(16):
        for _c2 in range(16):
            if _s <= _c2:
                TRIT[32 * _ci + _s, _c2] = 1.0


def build_C(NTB_lim=None, NG_lim=None, do_c4=True, do_fm=True, do_gate=True, do_tmst=True, do_epi=True, do_bc=True, do_E=True, debug=False):
    nc = bass.Bass("TRN2", target_bir_lowering=False)
    xnf = nc.dram_tensor("xnf", [D, TOK], BF16, kind="ExternalInput")
    glf = nc.dram_tensor("glf", [128, TOK], BF16, kind="ExternalInput")
    slab = nc.dram_tensor("slab", [D, 770], F32, kind="ExternalInput")
    wg2 = nc.dram_tensor("wg2", [128, 256], F32, kind="ExternalInput")
    lnwf = nc.dram_tensor("lnwf", [1, 256], F32, kind="ExternalInput")
    lnbf = nc.dram_tensor("lnbf", [1, 256], F32, kind="ExternalInput")
    nw2 = nc.dram_tensor("nw2", [128, 2], F32, kind="ExternalInput")
    og_o = nc.dram_tensor("og", [256, TOK], BF16, kind="ExternalOutput")

    trit_d = nc.inline_tensor(TRIT, name="trit")
    ident_d = nc.inline_tensor(IDENT, name="ident")
    ident32_d = nc.inline_tensor(IDENT32, name="ident32")

    kd = "ExternalOutput" if debug else "Internal"
    q_tm = nc.dram_tensor("q_tm", [TOK, 256], F32, kind=kd)
    k_tm = nc.dram_tensor("k_tm", [TOK, 256], F32, kind=kd)
    vt_tm = nc.dram_tensor("vt_tm", [TOK, 256], F32, kind=kd)
    sg_tm = nc.dram_tensor("sg_tm", [TOK, 256], BF16, kind=kd)
    o_tm = nc.dram_tensor("o_tm", [TOK, 256], BF16, kind=kd)
    q_fm = nc.dram_tensor("q_fm", [256, TOK], F32, kind=kd)
    k_fm = nc.dram_tensor("k_fm", [256, TOK], F32, kind=kd)
    beta_sc = nc.dram_tensor("beta_sc", [128, 128], F32, kind=kd)
    sdmp_d = nc.dram_tensor("sdmp", [128, 512], F32, kind=kd)
    khd_d = nc.dram_tensor("khd", [128, 544], F32, kind=kd)
    sdmp_sb = nc.alloc_sbuf_tensor("sdmp_sb", [128, 512], F32) if debug else None
    khd_sb = nc.alloc_sbuf_tensor("khd_sb", [128, 544], F32) if debug else None
    slab_b = nc.dram_tensor("slab_b", [D, 770], BF16, kind="Internal")
    wg2_b = nc.dram_tensor("wg2_b", [128, 256], BF16, kind="Internal")

    slab_sb = nc.alloc_sbuf_tensor("slab_sb", [128, 16 * 770], BF16)
    wg2_sb = nc.alloc_sbuf_tensor("wg2_sb", [128, 256], BF16)
    xnfb = [nc.alloc_sbuf_tensor(f"xnfb{i}", [128, D], BF16) for i in range(2)]
    glfb = [nc.alloc_sbuf_tensor(f"glfb{i}", [128, 128], BF16) for i in range(2)]
    tsig = nc.alloc_sbuf_tensor("tsig", [128, 256], BF16)
    sqk = nc.alloc_sbuf_tensor("sqk", [128, 256], BF16)
    tmq = nc.alloc_sbuf_tensor("tmq", [128, 256], F32)
    tmk = nc.alloc_sbuf_tensor("tmk", [128, 256], F32)
    tmvt = nc.alloc_sbuf_tensor("tmvt", [128, 256], F32)
    tmvtb = nc.alloc_sbuf_tensor("tmvtb", [128, 256], BF16)
    tmg = nc.alloc_sbuf_tensor("tmg", [128, 256], BF16)
    bet = nc.alloc_sbuf_tensor("bet", [128, 4], F32)
    beta_acc = nc.alloc_sbuf_tensor("beta_acc", [128, 128], F32)
    kst = nc.alloc_sbuf_tensor("kst", [128, 16], F32)
    cc = nc.alloc_sbuf_tensor("cc", [128, 8], F32)
    bbc = nc.alloc_sbuf_tensor("bbc", [128, 256], F32)
    lnw_sb = nc.alloc_sbuf_tensor("lnw_sb", [1, 256], F32)
    lnb_sb = nc.alloc_sbuf_tensor("lnb_sb", [1, 256], F32)
    nw_sb = nc.alloc_sbuf_tensor("nw_sb", [128, 2], F32)
    wbc = nc.alloc_sbuf_tensor("wbc", [128, 128], F32)
    bbc_st = nc.alloc_sbuf_tensor("bbc_st", [128, 128], F32)
    trit_sb = nc.alloc_sbuf_tensor("trit_sb", [128, 16], F32)
    id_sb = nc.alloc_sbuf_tensor("id_sb", [128, 128], BF16)
    id32_sb = nc.alloc_sbuf_tensor("id32_sb", [128, 128], F32)
    fmst = [nc.alloc_sbuf_tensor(f"fmst{i}", [128, 128], F32) for i in range(2)]
    E_sb = nc.alloc_sbuf_tensor("E_sb", [128, 256], F32)
    ones_sb = nc.alloc_sbuf_tensor("ones_sb", [1, 128], F32)
    qF = [nc.alloc_sbuf_tensor(f"qF{i}", [128, 512], F32) for i in range(2)]
    kF = [nc.alloc_sbuf_tensor(f"kF{i}", [128, 512], F32) for i in range(2)]
    qT = [nc.alloc_sbuf_tensor(f"qT{i}", [128, 1024], F32) for i in range(2)]
    kT = [nc.alloc_sbuf_tensor(f"kT{i}", [128, 1024], F32) for i in range(2)]
    vtT = [nc.alloc_sbuf_tensor(f"vtT{i}", [128, 1024], F32) for i in range(2)]
    S_sb = [nc.alloc_sbuf_tensor(f"S_sb{i}", [128, 128], F32) for i in range(4)]
    khh = nc.alloc_sbuf_tensor("khh", [128, 128], BF16)
    sqs = nc.alloc_sbuf_tensor("sqs", [128, 128], BF16)
    gg_ = nc.alloc_sbuf_tensor("gg_", [128, 128], F32)
    gw_ = nc.alloc_sbuf_tensor("gw_", [128, 128], F32)
    t2_ = nc.alloc_sbuf_tensor("t2_", [128, 128], F32)
    vm_ = nc.alloc_sbuf_tensor("vm_", [128, 128], BF16)
    at_f = nc.alloc_sbuf_tensor("at_f", [128, 16], F32)
    at_ = nc.alloc_sbuf_tensor("at_", [128, 16], BF16)
    kie_ = nc.alloc_sbuf_tensor("kie_", [128, 128], BF16)
    oh2b = nc.alloc_sbuf_tensor("oh2b", [128, 128], BF16)
    oh_ = nc.alloc_sbuf_tensor("oh_", [128, 128], F32)
    o_grp = nc.alloc_sbuf_tensor("o_grp", [128, 1024], BF16)
    oc4 = [nc.alloc_sbuf_tensor(f"oc4{i}", [128, 256], BF16) for i in range(2)]
    gc4 = [nc.alloc_sbuf_tensor(f"gc4{i}", [128, 256], BF16) for i in range(2)]
    ogt = nc.alloc_sbuf_tensor("ogt", [128, 256], BF16)
    ogn = nc.alloc_sbuf_tensor("ogn", [128, 256], BF16)
    ogf = [nc.alloc_sbuf_tensor(f"ogf{i}", [128, 128], BF16) for i in range(2)]

    ps = [nc.alloc_psum_tensor(f"ps{i}", [128, 512], F32) for i in range(8)]
    psb = [p.ap().bitcast(BF16) for p in ps]

    sch = Sched()
    bfree = {i: [] for i in range(8)}

    def wb(i):
        return list(bfree[i])

    def gapv():
        sch.op("vector", lambda e: e.tensor_copy(out=cc[:, 7:8], in_=cc[:, 6:7]))

    def gaps():
        sch.op("scalar", lambda e: e.activation(out=cc[:, 7:8], in_=cc[:, 6:7], func=AF.Copy))

    sch.op("gpsimd", lambda e: e.dma_start(out=slab_b[:, :], in_=slab[:, :]), inc="gld", inc_n=16)
    sch.op("gpsimd", lambda e: e.dma_start(out=wg2_b[:, :], in_=wg2[:, :]), inc="gld", inc_n=16)
    gld0 = sch.val("gld")
    sch.op("sync", lambda e: e.dma_start(
        out=slab_sb[:, :].rearrange("p (k j) -> p k j", j=770),
        in_=slab_b.ap().rearrange("(k p) j -> p k j", p=128)), waits=[("gld", gld0)], inc="ld", inc_n=16)
    sch.op("sync", lambda e: e.dma_start(out=wg2_sb[:, :], in_=wg2_b[:, :]),
           waits=[("gld", gld0)], inc="ld", inc_n=16)
    for dst, src in [(lnw_sb, lnwf), (lnb_sb, lnbf), (nw_sb, nw2), (trit_sb, trit_d),
                     (id_sb, ident_d), (id32_sb, ident32_d)]:
        sch.op("sync", lambda e, d=dst, s=src: e.dma_start(out=d[:, :], in_=s[:, :]),
               inc="ld", inc_n=16)
    ld_c0 = sch.val("ld")
    sch.op("vector", lambda e: e.memset(cc[:, 0:1], 1e-6))
    sch.op("vector", lambda e: e.memset(cc[:, 3:4], 1.0))
    sch.op("vector", lambda e: e.memset(ones_sb[:, :], 1.0))
    for ci4 in range(4):
        sch.op("vector", lambda e, i=ci4: e.memset(S_sb[i][:, :], 0.0))
    sch.op("vector", lambda e: e.memset(cc[:, 6:7], 0.5), inc="ve")
    ve_init = sch.val("ve")
    if not do_bc:
        ve_bc = sch.val("ve")
        bfree[0] = bfree[1] = bfree[2] = []
    if do_bc:
        _dummy = 0
    if do_bc:
        sch.op("tensor", lambda e: e.matmul(ps[0][:, 0:256], ones_sb[0:1, :], lnb_sb[0:1, :],
                                            start=True, stop=True),
               waits=[("ve", ve_init), ("ld", ld_c0)], inc="te")
        te_b = sch.val("te")
        sch.op("vector", lambda e: e.tensor_copy(out=bbc[:, :], in_=ps[0][:, 0:256]),
               waits=[("te", te_b)], inc="ve")
        for ci4 in range(4):
            hh = ci4 % 2
            tp = (0, 32 * ci4)
            sch.op("tensor", lambda e, c2=ci4, h=hh, t=tp: e.matmul(
                ps[1][32 * c2:32 * c2 + 16, 0:128], ones_sb[0:1, 0:16], lnw_sb[0:1, h * 128:(h + 1) * 128],
                start=True, stop=True, tile_position=t),
                waits=[("ve", ve_init), ("ld", ld_c0)], inc="te" if ci4 == 3 else None)
            sch.op("tensor", lambda e, c2=ci4, h=hh, t=tp: e.matmul(
                ps[2][32 * c2:32 * c2 + 16, 0:128], ones_sb[0:1, 0:16], lnb_sb[0:1, h * 128:(h + 1) * 128],
                start=True, stop=True, tile_position=t),
                waits=[], inc="te" if ci4 == 3 else None)
        te_wb = sch.val("te")
        sch.op("vector", lambda e: e.tensor_copy(out=wbc[:, :], in_=ps[1][:, 0:128]),
               waits=[("te", te_wb)])
        sch.op("vector", lambda e: e.tensor_copy(out=bbc_st[:, :], in_=ps[2][:, 0:128]), inc="ve")
        ve_bc = sch.val("ve")
        bfree[0] = bfree[1] = bfree[2] = [("ve", ve_bc)]

    NTB = NTB_lim or (TOK // 128)
    te_proj = {}
    marks = {}
    for tb in range(NTB):
        par = tb % 2
        bA = par * 2
        bB = par * 2 + 1
        wld = [("te", te_proj.get(tb - 2, 0))]
        sch.op("sync", lambda e, t=tb, p=par: e.dma_start(
            out=xnfb[p][:, :].rearrange("p2 (k j) -> p2 k j", j=128),
            in_=xnf.ap()[:, t * 128:(t + 1) * 128].rearrange("(k p2) j -> p2 k j", p2=128)),
            waits=[w for w in wld if w[1] > 0], inc="ld", inc_n=16)
        sch.op("sync", lambda e, t=tb, p=par: e.dma_start(
            out=glfb[p][:, :], in_=glf.ap()[:, t * 128:(t + 1) * 128]),
            inc="ld", inc_n=16)
        ld_tb = sch.val("ld")
        for k in range(16):
            w = [("ld", ld_tb)] + (wb(bA) if k == 0 else [])
            sch.op("tensor", lambda e, kk=k, p=par, b=bA: e.matmul(
                ps[b][:, :], xnfb[p][:, kk * 128:(kk + 1) * 128],
                slab_sb[:, kk * 770:kk * 770 + 512], start=(kk == 0), stop=(kk == 15)),
                waits=w)
            w2 = wb(bB) if k == 0 else []
            sch.op("tensor", lambda e, kk=k, p=par, b=bB: e.matmul(
                ps[b][:, 0:258], xnfb[p][:, kk * 128:(kk + 1) * 128],
                slab_sb[:, kk * 770 + 512:(kk + 1) * 770], start=(kk == 0), stop=(kk == 15)),
                waits=w2, inc="te" if k == 15 else None)
        te_p = sch.val("te")
        te_proj[tb] = te_p
        if do_gate:
            sch.op("tensor", lambda e, p=par: e.matmul(
                ps[7][:, 0:256], glfb[p][:, :], wg2_sb[:, :], start=True, stop=True),
                waits=[("ld", ld_tb)] + wb(7), inc="te")
        te_g = sch.val("te")

        m = marks.get(tb - 1, {})
        if not do_epi:
            bfree[bA] = [("te", te_p)]
            bfree[bB] = [("te", te_p)]
            marks[tb] = {}
            continue
        em = do_epi if isinstance(do_epi, int) and do_epi is not True else 15
        if em & 1:
            sch.op("scalar", lambda e, b=bA: e.activation(out=tsig[:, :], in_=ps[b][:, 0:256],
                                                          func=AF.Sigmoid),
                   waits=[("te", te_p)] + ([("ve", m["ve_q"])] if "ve_q" in m else []), inc="se")
        se_q = sch.val("se")
        if em & 2:
            sch.op("scalar", lambda e, b=bA: e.activation(out=sqk[:, :], in_=ps[b][:, 256:512],
                                                          func=AF.Square),
                   waits=[("te", te_p)] + ([("ve", m["ve_red"])] if "ve_red" in m else []), inc="se")
        se_k = sch.val("se")
        if em & 8:
            sch.op("scalar", lambda e, b=bB: e.activation(out=bet[:, 0:2], in_=ps[b][:, 256:258],
                                                          func=AF.Sigmoid),
                   waits=[("te", te_p)] + ([("ve", m["ve_bet"])] if "ve_bet" in m else []), inc="se")
        se_b = sch.val("se")
        if do_gate:
            sch.op("scalar", lambda e: e.activation(out=tmg[:, :], in_=ps[7][:, 0:256],
                                                    func=AF.Sigmoid),
                   waits=[("te", te_g)] + ([("stq", m["st_g"])] if "st_g" in m else []), inc="se")
        se_g = sch.val("se")
        bfree[7] = [("se", se_g)]

        if em & 1:
            sch.op("vector", lambda e, b=bA: e.tensor_mul(out=tmq[:, :], in0=tsig[:, :],
                                                          in1=ps[b][:, 0:256]),
                   waits=[("se", se_k)] + ([("te", m["te_tr"]), ("stq", m["st_q"])] if "te_tr" in m else []),
                   inc="ve")
        ve_q = sch.val("ve")
        if em & 2:
            sch.op("vector", lambda e: e.reduce_sum(out=kst[:, 0:1], in_=sqk[:, 0:128], axis=AX.X),
                   waits=[("se", se_k)])
            sch.op("vector", lambda e: e.reduce_sum(out=kst[:, 1:2], in_=sqk[:, 128:256], axis=AX.X))
            sch.op("vector", lambda e: e.tensor_copy(out=cc[:, 7:8], in_=cc[:, 6:7]))
            sch.op("vector", lambda e: e.tensor_scalar(out=kst[:, 0:2], in0=kst[:, 0:2],
                                                       scalar1=1e-6, scalar2=None, op0=ALU.add),
                   inc="ve")
        ve_red = sch.val("ve")
        if em & 2:
            sch.op("vector", lambda e: e.tensor_copy(out=cc[:, 7:8], in_=cc[:, 6:7]))
            newton_rsqrt(sch, "vector", kst, 0, 2, 4, cc)
            newton_rsqrt(sch, "vector", kst, 1, 3, 4, cc)
            for hh in range(2):
                sch.op("vector", lambda e, b=bA, h=hh: e.tensor_scalar(
                    out=tmk[:, h * 128:(h + 1) * 128], in0=ps[b][:, 256 + h * 128:256 + (h + 1) * 128],
                    scalar1=kst[:, 2 + h:3 + h], scalar2=None, op0=ALU.mult),
                    waits=[("te", m["te_tr"]), ("stq", m["st_k"])] if (hh == 0 and "te_tr" in m) else [],
                    inc="ve" if hh == 1 else None)
        ve_k = sch.val("ve")
        if em & 4:
            sch.op("vector", lambda e, b=bB: e.tensor_sub(out=tmvt[:, :], in0=bbc[:, :],
                                                          in1=ps[b][:, 0:256]),
                   waits=[("te", te_p), ("se", se_b)] + ([("ve", ve_bc)] if tb == 0 else []))
            sch.op("vector", lambda e: e.tensor_add(out=tmvt[:, :], in0=tmvt[:, :], in1=tmk[:, :]),
                   waits=[("stq", m["st_vt"])] if "st_vt" in m else [], inc="ve")
        ve_vt = sch.val("ve")
        bfree[bB] = [("ve", ve_vt), ("se", se_b)]
        bfree[bA] = [("ve", ve_vt)]
        if em & 8:
            sch.op("vector", lambda e, t=tb: e.tensor_scalar(
                out=beta_acc[:, t:t + 1], in0=bet[:, 0:1], scalar1=2.0, scalar2=None, op0=ALU.mult),
                waits=[("se", se_b)])
            sch.op("vector", lambda e, t=tb: e.tensor_scalar(
                out=beta_acc[:, 64 + t:64 + t + 1], in0=bet[:, 1:2], scalar1=2.0, scalar2=None,
                op0=ALU.mult), inc="ve")
        ve_bet = sch.val("ve")

        st_fm_first = None
        for j in (range(4) if do_fm else []):
            hh = j % 2
            srcbuf = tmq if j < 2 else tmk
            wtr = [("ve", ve_q if j < 2 else ve_k)] + wb(6)
            sch.op("tensor", lambda e, s=srcbuf, h=hh: e.transpose(
                ps[6][:, 0:128], s[:, h * 128:(h + 1) * 128], id32_sb[:, :]),
                waits=wtr, inc="te")
            tev = sch.val("te")
            sch.op("vector", lambda e, jj=j: e.tensor_copy(out=fmst[jj % 2][:, :],
                                                           in_=ps[6][:, 0:128]),
                   waits=[("te", tev)] + ([("stq", sch.val("stq") - 16)] if j >= 2 or tb > 0 else []),
                   inc="ve")
            vev = sch.val("ve")
            bfree[6] = [("ve", vev)]
            dstd = q_fm if j < 2 else k_fm
            sch.op("sync", lambda e, t=tb, h=hh, d=dstd, jj=j: e.dma_start(
                out=d.ap()[h * 128:(h + 1) * 128, t * 128:(t + 1) * 128], in_=fmst[jj % 2][:, :]),
                waits=[("ve", vev)], inc="stq", inc_n=16)
        te_tr = sch.val("te")
        if not do_tmst:
            marks[tb] = dict(ve_q=ve_q, ve_red=ve_red, ve_k=ve_k, ve_vt=ve_vt, ve_bet=ve_bet)
            if do_fm:
                marks[tb]["te_tr"] = te_tr
                marks[tb]["st_q"] = sch.val("stq")
                marks[tb]["st_k"] = sch.val("stq")
                marks[tb]["st_vt"] = sch.val("stq")
            continue
        sch.op("sync", lambda e, t=tb: e.dma_start(out=q_tm[t * 128:(t + 1) * 128, :], in_=tmq[:, :]),
               waits=[("ve", ve_q)], inc="stq", inc_n=16)
        st_q = sch.val("stq")
        sch.op("sync", lambda e, t=tb: e.dma_start(out=k_tm[t * 128:(t + 1) * 128, :], in_=tmk[:, :]),
               waits=[("ve", ve_k)], inc="stq", inc_n=16)
        st_k = sch.val("stq")
        sch.op("sync", lambda e, t=tb: e.dma_start(out=vt_tm[t * 128:(t + 1) * 128, :], in_=tmvt[:, :]),
               waits=[("ve", ve_vt)], inc="stq", inc_n=16)
        st_vt = sch.val("stq")
        sch.op("sync", lambda e, t=tb: e.dma_start(out=sg_tm[t * 128:(t + 1) * 128, :], in_=tmg[:, :]),
               waits=[("se", se_g)], inc="stq", inc_n=16)
        st_g = sch.val("stq")
        marks[tb] = dict(ve_q=ve_q, ve_red=ve_red, ve_k=ve_k, ve_vt=ve_vt, ve_bet=ve_bet,
                         te_tr=te_tr, st_q=st_q, st_k=st_k, st_vt=st_vt, st_g=st_g)

    sch.op("sync", lambda e: e.dma_start(out=beta_sc[:, :], in_=beta_acc[:, :]),
           waits=[("ve", sch.val("ve"))], inc="stq", inc_n=16)
    st_beta = sch.val("stq")

    bview = beta_sc.ap().rearrange("(sb j) (h b2 sa) -> j sb h b2 sa", j=16, h=2, b2=2)
    for ci4 in (range(4) if do_E else []):
        b_ = ci4 // 2
        hh = ci4 % 2
        sch.op("sync", lambda e, c2=ci4, h=hh, bb2=b_: e.dma_start(
            out=E_sb[32 * c2:32 * c2 + 16, :].rearrange("p (sb sa) -> p sb sa", sa=32),
            in_=bview[:, :, h:h + 1, bb2:bb2 + 1, :].rearrange("j sb x y sa -> j sb (x y sa)")),
            waits=[("stq", st_beta)], inc="ld", inc_n=16)
    ld_E = sch.val("ld")

    NG_ = NG_lim if NG_lim is not None else 32
    gmark = {}
    smark = {}
    st_o = {}
    for g in range(NG_):
        par = g % 2
        pm = gmark.get(g - 2, {})
        wldg = ([("te", pm["te2"]), ("ve", pm["ve5"])] if pm else [("stq", st_beta)])
        for ci4 in range(4):
            b_ = ci4 // 2
            hh = ci4 % 2
            tok0 = b_ * T + g * 128
            sch.op("sync", lambda e, c2=ci4, h=hh, t0=tok0, p=par: e.dma_start(
                out=qF[p][:, 128 * c2:128 * (c2 + 1)],
                in_=q_fm.ap()[h * 128:(h + 1) * 128, t0:t0 + 128]),
                waits=wldg if ci4 == 0 else [], inc="ld", inc_n=16)
            sch.op("sync", lambda e, c2=ci4, h=hh, t0=tok0, p=par: e.dma_start(
                out=kF[p][:, 128 * c2:128 * (c2 + 1)],
                in_=k_fm.ap()[h * 128:(h + 1) * 128, t0:t0 + 128]),
                inc="ld", inc_n=16)
            for dstT, srcd in [(qT, q_tm), (kT, k_tm), (vtT, vt_tm)]:
                sch.op("sync", lambda e, c2=ci4, h=hh, t0=tok0, p=par, dT=dstT, sd=srcd:
                       e.dma_start(
                           out=dT[p][32 * c2:32 * c2 + 16, :].rearrange("p2 (sb d) -> p2 sb d", d=128),
                           in_=sd.ap()[t0:t0 + 128, h * 128:(h + 1) * 128]
                               .rearrange("(sb j) d -> j sb d", j=16)),
                       inc="ld", inc_n=16)
        ld_g = sch.val("ld")
        gmark[g] = {"ld": ld_g}

        for s2 in range(8):
            s = g * 8 + s2
            scol = (s % 8) * 32 + s // 8
            prev = smark.get(s - 1, {})
            w1 = [("ld", ld_g)]
            if prev:
                w1 += [("ve", prev["ve5"]), ("se", prev["se4"])]
            elif s == 0:
                w1 += [("ve", ve_init), ("ld", ld_E)]
            for ci4 in range(4):
                tp = (0, 32 * ci4)
                sch.op("tensor", lambda e, c2=ci4, p=par, ss=s2, t=tp: e.matmul(
                    ps[0][32 * c2:32 * c2 + 16, 0:128],
                    kF[p][:, 128 * c2 + 16 * ss:128 * c2 + 16 * ss + 16],
                    S_sb[c2][:, :], start=True, stop=True, tile_position=t),
                    waits=w1 if ci4 == 0 else [])
            for ci4 in range(4):
                tp = (0, 32 * ci4)
                sch.op("tensor", lambda e, c2=ci4, p=par, ss=s2, t=tp: e.matmul(
                    ps[2][32 * c2:32 * c2 + 16, 0:16],
                    kF[p][:, 128 * c2 + 16 * ss:128 * c2 + 16 * ss + 16],
                    qF[p][:, 128 * c2 + 16 * ss:128 * c2 + 16 * ss + 16],
                    start=True, stop=True, tile_position=t))
            w3 = [("ve", prev["ve5"]), ("se", prev["se4"])] if prev else []
            for ci4 in range(4):
                tp = (0, 32 * ci4)
                sch.op("tensor", lambda e, c2=ci4, p=par, ss=s2, t=tp: e.matmul(
                    ps[1][32 * c2:32 * c2 + 16, 0:128],
                    qF[p][:, 128 * c2 + 16 * ss:128 * c2 + 16 * ss + 16],
                    S_sb[c2][:, :], start=True, stop=False, tile_position=t),
                    waits=w3 if ci4 == 0 else [], inc="te" if ci4 == 3 else None)
            te1 = sch.val("te")
            sch.op("scalar", lambda e: e.activation(out=sqs[:, :], in_=ps[0][:, 0:128],
                                                    func=AF.Square, accum_out=kst[:, 1:2]),
                   waits=[("te", te1)])
            sch.op("scalar", lambda e: e.activation(out=khh[:, :], in_=ps[0][:, 0:128],
                                                    func=AF.Copy, accum_out=kst[:, 0:1]))
            gaps()
            sch.op("scalar", lambda e: e.activation(out=cc[:, 7:8], in_=cc[:, 6:7],
                                                    func=AF.Copy), inc="se")
            se1 = sch.val("se")
            sch.op("vector", lambda e: e.tensor_scalar_mul(out=kst[:, 4:5], in0=kst[:, 0:1],
                                                           scalar1=1.0 / 128),
                   waits=[("se", se1)])
            sch.op("vector", lambda e: e.tensor_scalar_mul(out=kst[:, 5:6], in0=kst[:, 1:2],
                                                           scalar1=1.0 / 128))
            sch.op("vector", lambda e: e.tensor_mul(out=kst[:, 6:7], in0=kst[:, 4:5], in1=kst[:, 4:5]))
            gapv()
            sch.op("vector", lambda e: e.tensor_sub(out=kst[:, 6:7], in0=kst[:, 5:6], in1=kst[:, 6:7]))
            gapv()
            sch.op("vector", lambda e: e.tensor_copy(out=cc[:, 7:8], in_=cc[:, 6:7]), inc="ve")
            ve1 = sch.val("ve")
            sch.op("scalar", lambda e: e.activation(out=kst[:, 7:8], in_=kst[:, 6:7],
                                                    func=AF.Sqrt, bias=cc[:, 0:1]),
                   waits=[("ve", ve1)])
            sch.op("scalar", lambda e: e.activation(out=cc[:, 7:8], in_=cc[:, 6:7],
                                                    func=AF.Copy), inc="se")
            se2 = sch.val("se")
            sch.op("vector", lambda e: e.reciprocal(out=kst[:, 8:9], in_=kst[:, 7:8]),
                   waits=[("se", se2)])
            gapv()
            sch.op("vector", lambda e: e.tensor_scalar(out=khh[:, :], in0=ps[0][:, 0:128],
                                                       scalar1=kst[:, 4:5], scalar2=kst[:, 8:9],
                                                       op0=ALU.subtract, op1=ALU.mult))
            sch.op("vector", lambda e, sc2=scol: e.tensor_scalar(
                out=at_f[:, :], in0=ps[2][:, 0:16], scalar1=E_sb[:, sc2:sc2 + 1],
                scalar2=None, op0=ALU.mult))
            sch.op("vector", lambda e: e.tensor_mul(out=gg_[:, :], in0=khh[:, :], in1=wbc[:, :]))
            sch.op("vector", lambda e, sc2=scol, p=par, ss=s2: e.tensor_scalar(
                out=kie_[:, :], in0=kT[p][:, 128 * ss:128 * (ss + 1)],
                scalar1=E_sb[:, sc2:sc2 + 1], scalar2=None, op0=ALU.mult))
            sch.op("vector", lambda e, p=par, ss=s2: e.tensor_add(
                out=gg_[:, :], in0=gg_[:, :], in1=vtT[p][:, 128 * ss:128 * (ss + 1)]))
            sch.op("vector", lambda e: e.tensor_mul(out=at_[:, :], in0=at_f[:, :], in1=trit_sb[:, :]))
            sch.op("vector", lambda e: e.tensor_mul(out=gw_[:, :], in0=gg_[:, :], in1=wbc[:, :]))
            gapv()
            sch.op("vector", lambda e: e.tensor_mul(out=t2_[:, :], in0=gw_[:, :], in1=khh[:, :]))
            sch.op("vector", lambda e: e.reduce_sum(out=kst[:, 11:12], in_=gw_[:, :], axis=AX.X))
            sch.op("vector", lambda e: e.reduce_sum(out=kst[:, 12:13], in_=t2_[:, :], axis=AX.X))
            sch.op("vector", lambda e: e.tensor_scalar_mul(out=kst[:, 11:12], in0=kst[:, 11:12],
                                                           scalar1=1.0 / 128))
            sch.op("vector", lambda e: e.tensor_scalar_mul(out=kst[:, 12:13], in0=kst[:, 12:13],
                                                           scalar1=1.0 / 128))
            gapv()
            sch.op("vector", lambda e: e.tensor_scalar(out=t2_[:, :], in0=khh[:, :],
                                                       scalar1=kst[:, 12:13], scalar2=None,
                                                       op0=ALU.mult))
            gapv()
            sch.op("vector", lambda e: e.tensor_sub(out=gg_[:, :], in0=t2_[:, :], in1=gw_[:, :]))
            gapv()
            sch.op("vector", lambda e: e.tensor_scalar(out=vm_[:, :], in0=gg_[:, :],
                                                       scalar1=kst[:, 11:12], scalar2=kst[:, 8:9],
                                                       op0=ALU.add, op1=ALU.mult))
            if debug and s == 1:
                sch.op("vector", lambda e: e.tensor_copy(out=khd_sb[:, 128:144], in_=kst[:, :]))
                sch.op("vector", lambda e: e.tensor_copy(out=khd_sb[:, 144:272], in_=khh[:, :]))
                sch.op("vector", lambda e: e.tensor_copy(out=khd_sb[:, 272:400], in_=vm_[:, :]))
                sch.op("vector", lambda e: e.tensor_copy(out=khd_sb[:, 528:544], in_=at_[:, :]))
            sch.op("vector", lambda e: e.tensor_copy(out=cc[:, 7:8], in_=cc[:, 6:7]), inc="ve")
            ve3 = sch.val("ve")
            for ci4 in range(4):
                tp = (32 * ci4, 32 * ci4)
                sch.op("tensor", lambda e, c2=ci4, t=tp: e.matmul(
                    ps[1][32 * c2:32 * c2 + 16, 0:128], at_[32 * c2:32 * c2 + 16, :],
                    vm_[32 * c2:32 * c2 + 16, :], start=False, stop=True, tile_position=t),
                    waits=[("ve", ve3)] if ci4 == 0 else [])
            for ci4 in range(4):
                tp = (32 * ci4, 0)
                sch.op("tensor", lambda e, c2=ci4, ss=s, t=tp: e.matmul(
                    ps[4 + c2][:, 0:128], kie_[32 * c2:32 * c2 + 16, :],
                    vm_[32 * c2:32 * c2 + 16, :], start=(ss == 0), stop=(ss == 255),
                    tile_position=t), inc="te" if ci4 == 3 else None)
            te2 = sch.val("te")
            sch.op("scalar", lambda e: e.activation(out=sqs[:, :], in_=ps[1][:, 0:128],
                                                    func=AF.Square, accum_out=kst[:, 2:3]),
                   waits=[("te", te2)])
            sch.op("scalar", lambda e: e.activation(out=khh[:, :], in_=ps[1][:, 0:128],
                                                    func=AF.Copy, accum_out=kst[:, 3:4]))
            sch.op("scalar", lambda e: e.activation(out=S_sb[0][:, :], in_=ps[4][:, 0:128],
                                                    func=AF.Copy))
            sch.op("scalar", lambda e: e.activation(out=S_sb[1][:, :], in_=ps[5][:, 0:128],
                                                    func=AF.Copy))
            sch.op("scalar", lambda e: e.activation(out=cc[:, 7:8], in_=cc[:, 6:7],
                                                    func=AF.Copy), inc="se")
            se4 = sch.val("se")
            if debug and s == 1:
                sch.op("vector", lambda e: e.tensor_copy(out=khd_sb[:, 400:528], in_=ps[1][:, 0:128]),
                       waits=[("se", se4)])
            sch.op("vector", lambda e: e.tensor_copy(out=S_sb[2][:, :], in_=ps[6][:, 0:128]),
                   waits=[("se", se4)])
            sch.op("vector", lambda e: e.tensor_copy(out=S_sb[3][:, :], in_=ps[7][:, 0:128]))
            sch.op("vector", lambda e: e.tensor_scalar_mul(out=kst[:, 4:5], in0=kst[:, 3:4],
                                                           scalar1=1.0 / 128))
            sch.op("vector", lambda e: e.tensor_scalar_mul(out=kst[:, 5:6], in0=kst[:, 2:3],
                                                           scalar1=1.0 / 128))
            sch.op("vector", lambda e: e.tensor_mul(out=kst[:, 6:7], in0=kst[:, 4:5], in1=kst[:, 4:5]))
            gapv()
            sch.op("vector", lambda e: e.tensor_sub(out=kst[:, 6:7], in0=kst[:, 5:6], in1=kst[:, 6:7]))
            sch.op("vector", lambda e: e.tensor_copy(out=cc[:, 7:8], in_=cc[:, 6:7]), inc="ve")
            ve4 = sch.val("ve")
            sch.op("scalar", lambda e: e.activation(out=kst[:, 7:8], in_=kst[:, 6:7],
                                                    func=AF.Sqrt, bias=cc[:, 0:1]),
                   waits=[("ve", ve4)])
            sch.op("scalar", lambda e: e.activation(out=cc[:, 7:8], in_=cc[:, 6:7],
                                                    func=AF.Copy), inc="se")
            se5 = sch.val("se")
            sch.op("vector", lambda e: e.reciprocal(out=kst[:, 8:9], in_=kst[:, 7:8]),
                   waits=[("se", se5)])
            gapv()
            sch.op("vector", lambda e: e.tensor_scalar(out=oh2b[:, :], in0=ps[1][:, 0:128],
                                                       scalar1=kst[:, 4:5], scalar2=kst[:, 8:9],
                                                       op0=ALU.subtract, op1=ALU.mult))
            gapv()
            sch.op("vector", lambda e: e.tensor_mul(out=oh_[:, :], in0=oh2b[:, :], in1=wbc[:, :]))
            gapv()
            sch.op("vector", lambda e: e.tensor_add(out=oh_[:, :], in0=oh_[:, :], in1=bbc_st[:, :]))
            gapv()
            wog = [("stq", st_o[g - 1])] if (s2 == 0 and g >= 1) else []
            sch.op("vector", lambda e, p=par, ss=s2: e.tensor_add(
                out=o_grp[:, 128 * ss:128 * (ss + 1)], in0=oh_[:, :],
                in1=qT[p][:, 128 * ss:128 * (ss + 1)]), waits=wog)
            sch.op("vector", lambda e: e.tensor_copy(out=cc[:, 7:8], in_=cc[:, 6:7]), inc="ve")
            ve5 = sch.val("ve")
            if debug and s in (0, 1):
                off = 0 if s == 0 else 256
                for ci4 in range(2):
                    sch.op("vector", lambda e, c2=ci4, o4=off: e.tensor_copy(
                        out=sdmp_sb[:, o4 + 128 * c2:o4 + 128 * (c2 + 1)], in_=S_sb[c2][:, :]),
                        waits=[("ve", ve4), ("se", se4)], inc="ve" if ci4 == 1 else None)
                if s == 1:
                    sch.op("sync", lambda e: e.dma_start(out=sdmp_d[:, :], in_=sdmp_sb[:, :]),
                           waits=[("ve", sch.val("ve"))], inc="stq", inc_n=16)
            if debug and s == 1:
                sch.op("vector", lambda e: e.tensor_copy(out=khd_sb[:, 0:128], in_=ps[0][:, 0:128]),
                       waits=[("se", se1)])
                sch.op("vector", lambda e: e.tensor_copy(out=cc[:, 7:8], in_=cc[:, 6:7]), inc="ve")
                sch.op("sync", lambda e: e.dma_start(out=khd_d[:, :], in_=khd_sb[:, :]),
                       waits=[("ve", sch.val("ve"))], inc="stq", inc_n=16)
            ve4_eff = sch.val("ve") if (debug and s in (0, 1)) else ve4
            smark[s] = dict(se4=se4, ve4=ve4_eff, ve5=ve5)
        gmark[g]["te2"] = sch.val("te")
        gmark[g]["ve5"] = smark[g * 8 + 7]["ve5"]
        for ci4 in range(4):
            b_ = ci4 // 2
            hh = ci4 % 2
            tok0 = b_ * T + g * 128
            sch.op("sync", lambda e, c2=ci4, h=hh, t0=tok0: e.dma_start(
                out=o_tm.ap()[t0:t0 + 128, h * 128:(h + 1) * 128]
                    .rearrange("(sb j) d -> j sb d", j=16),
                in_=o_grp[32 * c2:32 * c2 + 16, :].rearrange("p2 (sb d) -> p2 sb d", d=128)),
                waits=[("ve", gmark[g]["ve5"])] if ci4 == 0 else [], inc="stq", inc_n=16)
        st_o[g] = sch.val("stq")

    c4m = {}
    for tb in (range(NTB) if do_c4 else []):
        par = tb % 2
        pm = c4m.get(tb - 2, {})
        wld = [("ve", pm["ve"])] if pm else [("stq", st_o[NG_ - 1] if NG_ > 0 else st_beta)]
        b_ = (tb * 128) // T
        trow = tb * 128
        sch.op("sync", lambda e, t=trow, p=par: e.dma_start(out=oc4[p][:, :], in_=o_tm[t:t + 128, :]),
               waits=wld, inc="ld", inc_n=16)
        sch.op("sync", lambda e, t=trow, p=par: e.dma_start(out=gc4[p][:, :], in_=sg_tm[t:t + 128, :]),
               inc="ld", inc_n=16)
        ld4 = sch.val("ld")
        pm1 = c4m.get(tb - 1, {})
        sch.op("vector", lambda e, p=par: e.tensor_mul(out=ogt[:, :], in0=oc4[p][:, :],
                                                       in1=gc4[p][:, :]),
               waits=[("ld", ld4)] + ([("te", pm1["te"])] if pm1 else []), inc="ve")
        ve_og = sch.val("ve")
        for hh in range(2):
            sch.op("scalar", lambda e, h=hh: e.activation(
                out=sqs[:, :], in_=ogt[:, h * 128:(h + 1) * 128], func=AF.Square,
                accum_out=kst[:, h:h + 1]),
                waits=[("ve", ve_og)] if hh == 0 else [])
        sch.op("scalar", lambda e: e.activation(out=cc[:, 7:8], in_=cc[:, 6:7], func=AF.Copy))
        sch.op("scalar", lambda e: e.activation(out=kst[:, 2:4], in_=kst[:, 0:2], func=AF.Sqrt,
                                                scale=1.0 / 128, bias=cc[:, 0:1]))
        sch.op("scalar", lambda e: e.activation(out=cc[:, 7:8], in_=cc[:, 6:7], func=AF.Copy),
               inc="se")
        se_r = sch.val("se")
        sch.op("vector", lambda e: e.reciprocal(out=kst[:, 4:6], in_=kst[:, 2:4]),
               waits=[("se", se_r)])
        gapv()
        for hh in range(2):
            sch.op("vector", lambda e, h=hh: e.tensor_scalar(
                out=ogn[:, h * 128:(h + 1) * 128], in0=ogt[:, h * 128:(h + 1) * 128],
                scalar1=kst[:, 4 + h:5 + h], scalar2=None, op0=ALU.mult),
                inc="ve" if hh == 1 else None)
        ve_n = sch.val("ve")
        if debug and tb == 0:
            sch.op("vector", lambda e: e.tensor_copy(out=sdmp_sb[:, 384:512], in_=ogn[:, 0:128]))
            sch.op("vector", lambda e: e.tensor_copy(out=khd_sb[:, 0:128], in_=ogt[:, 0:128]))
            sch.op("vector", lambda e: e.tensor_copy(out=khd_sb[:, 128:144], in_=kst[:, :]), inc="ve")
            ve_n = sch.val("ve")
        for hh in range(2):
            sch.op("tensor", lambda e, h=hh: e.transpose(
                psb[6][:, 0:128], ogn[:, h * 128:(h + 1) * 128], id_sb[:, :]),
                waits=[("ve", ve_n)] + wb(6), inc="te")
            tev = sch.val("te")
            sch.op("vector", lambda e, h=hh: e.tensor_scalar(
                out=ogf[h][:, :], in0=psb[6][:, 0:128], scalar1=nw_sb[:, h:h + 1],
                scalar2=None, op0=ALU.mult),
                waits=[("te", tev)] + ([("stq", sch.val("stq") - 16)] if (tb > 0 or hh > 0) else []),
                inc="ve")
            vev = sch.val("ve")
            bfree[6] = [("ve", vev)]
            sch.op("sync", lambda e, t=trow, h=hh: e.dma_start(
                out=og_o[h * 128:(h + 1) * 128, t:t + 128], in_=ogf[h][:, :]),
                waits=[("ve", vev)], inc="stq", inc_n=16)
        c4m[tb] = dict(ve=sch.val("ve"), te=sch.val("te"))

    sch.op("sync", lambda e: e.dma_start(out=beta_sc[0:1, 0:1], in_=beta_sc[0:1, 1:2]),
           waits=[("stq", sch.val("stq"))], inc="ld", inc_n=16)

    with ExitStack() as stack:
        sems = make_sems(nc, stack, ["ld", "gld", "te", "ve", "se", "stq"])
        sch.emit(nc, sems)
    return nc


def newton_rsqrt(sch, eng, buf, vcol, ycol, tmpcol, cc):
    MAGIC = 0x5f3759df
    bi = buf[:, :].bitcast(I32)

    def gap():
        sch.op(eng, lambda e: e.tensor_copy(out=cc[:, 7:8], in_=cc[:, 6:7]))

    sch.op(eng, lambda e: e.tensor_scalar(out=bi[:, ycol:ycol + 1], in0=bi[:, vcol:vcol + 1],
                                          scalar1=1, scalar2=None, op0=ALU.logical_shift_right))
    gap()
    sch.op(eng, lambda e: e.tensor_scalar(out=bi[:, ycol:ycol + 1], in0=bi[:, ycol:ycol + 1],
                                          scalar1=-1, scalar2=MAGIC, op0=ALU.mult, op1=ALU.add))
    gap()
    for _ in range(2):
        sch.op(eng, lambda e: e.tensor_mul(out=buf[:, tmpcol:tmpcol + 1], in0=buf[:, vcol:vcol + 1],
                                           in1=buf[:, ycol:ycol + 1]))
        gap()
        sch.op(eng, lambda e: e.tensor_mul(out=buf[:, tmpcol:tmpcol + 1],
                                           in0=buf[:, tmpcol:tmpcol + 1], in1=buf[:, ycol:ycol + 1]))
        gap()
        sch.op(eng, lambda e: e.tensor_scalar(out=buf[:, tmpcol:tmpcol + 1],
                                              in0=buf[:, tmpcol:tmpcol + 1], scalar1=-0.5,
                                              scalar2=1.5, op0=ALU.mult, op1=ALU.add))
        gap()
        sch.op(eng, lambda e: e.tensor_mul(out=buf[:, ycol:ycol + 1], in0=buf[:, ycol:ycol + 1],
                                           in1=buf[:, tmpcol:tmpcol + 1]))
        gap()


def run_C(inputs, a_out, debug=False):
    xnf = np.concatenate([np.asarray(a_out[c]["xnT"]) for c in range(NCORES)], axis=1)
    glf = np.concatenate([np.asarray(a_out[c]["gl"]) for c in range(NCORES)], axis=1)
    Wq = np.asarray(inputs["L0_attn_Wq"], dtype=np.float32)
    Wk = np.asarray(inputs["L0_attn_Wk"], dtype=np.float32)
    Wv = np.asarray(inputs["L0_attn_Wv"], dtype=np.float32)
    Wb = np.asarray(inputs["L0_attn_Wbeta"], dtype=np.float32)
    Wg2 = np.asarray(inputs["L0_attn_Wg2"], dtype=np.float32)
    lnw = np.asarray(inputs["L0_attn_ln_w"], dtype=np.float32)
    lnb = np.asarray(inputs["L0_attn_ln_b"], dtype=np.float32)
    nw = np.asarray(inputs["L0_attn_norm_w"], dtype=np.float32)
    in_maps = []
    for c in range(NCORES):
        cols = slice(256 * c, 256 * c + 256)
        h2 = slice(2 * c, 2 * c + 2)
        slab_np = np.ascontiguousarray(np.concatenate(
            [Wq[:, cols], Wk[:, cols], Wv[:, cols], Wb[:, h2]], axis=1))
        in_maps.append({
            "xnf": xnf, "glf": glf, "slab": slab_np,
            "wg2": np.ascontiguousarray(Wg2[:, cols]),
            "lnwf": np.ascontiguousarray(lnw[h2].reshape(1, 256)),
            "lnbf": np.ascontiguousarray(lnb[h2].reshape(1, 256)),
            "nw2": np.ascontiguousarray(nw[cols].reshape(2, 128).T),
        })
    nc = build_C(debug=debug)
    res = run_bass_kernel_spmd(nc, in_maps, core_ids=list(range(NCORES)))
    return res.results



def build_D():
    nc = bass.Bass("TRN2", target_bir_lowering=False)
    x_d = nc.dram_tensor("x", [TPC, D], F32, kind="ExternalInput")
    ogl_d = nc.dram_tensor("ogl", [D, TPC], BF16, kind="ExternalInput")
    Wo_d = nc.dram_tensor("Wo", [D, D], F32, kind="ExternalInput")
    y_dd = nc.dram_tensor("ymoe", [TPC, D], F32, kind="ExternalInput")
    kva_d = nc.dram_tensor("kva", [D, 512], F32, kind="ExternalInput")
    kvb_d = nc.dram_tensor("kvb", [512, 512], F32, kind="ExternalInput")
    x1_o = nc.dram_tensor("x1", [TPC, D], F32, kind="ExternalOutput")
    xn1T_o = nc.dram_tensor("xn1T", [D, TPC], BF16, kind="ExternalOutput")
    kvf_o = nc.dram_tensor("kvf", [512, TPC], BF16, kind="ExternalOutput")

    bWo = nc.dram_tensor("bWo", [D, D], BF16, kind="Internal")
    bkva = nc.dram_tensor("bkva", [D, 512], BF16, kind="Internal")
    bkvb = nc.dram_tensor("bkvb", [512, 512], BF16, kind="Internal")
    ident_d = nc.inline_tensor(IDENT, name="identD")

    ogl = nc.alloc_sbuf_tensor("ogl_sb", [128, 16 * TPC], BF16)
    wo_sb = nc.alloc_sbuf_tensor("wo_sbD", [128, 16 * 2048], BF16)
    xsb = nc.alloc_sbuf_tensor("xsb", [128, D], F32)
    ysb = nc.alloc_sbuf_tensor("ysb", [128, D], F32)
    x1sb = nc.alloc_sbuf_tensor("x1sb", [128, D], F32)
    xn1b = nc.alloc_sbuf_tensor("xn1b", [128, D], BF16)
    xn1T_sb = nc.alloc_sbuf_tensor("xn1T_sb", [128, 16 * TPC], BF16)
    sq = nc.alloc_sbuf_tensor("sqD", [128, D], F32)
    st_ = nc.alloc_sbuf_tensor("stD", [128, 8], F32)
    cst = nc.alloc_sbuf_tensor("cstD", [128, 2], F32)
    id_sb = nc.alloc_sbuf_tensor("id_sbD", [128, 128], BF16)
    kva_sb = nc.alloc_sbuf_tensor("kva_sb", [128, 16 * 512], BF16)
    kvb_sb = nc.alloc_sbuf_tensor("kvb_sb", [128, 4 * 512], BF16)
    kvc_sb = nc.alloc_sbuf_tensor("kvc_sb", [128, 4 * TPC], BF16)
    kvf_sb = nc.alloc_sbuf_tensor("kvf_sb", [128, 4 * TPC], BF16)
    ev = [nc.alloc_sbuf_tensor(f"evD{i}", [128, 128], BF16) for i in range(2)]

    ps = [nc.alloc_psum_tensor(f"psD{i}", [128, 512], F32) for i in range(8)]

    sch = Sched()
    bfree = {i: [] for i in range(8)}

    def wb(i):
        return list(bfree[i])

    sch.op("gpsimd", lambda e: e.dma_start(out=bWo[:, :], in_=Wo_d[:, :]), inc="gld", inc_n=16)
    sch.op("gpsimd", lambda e: e.dma_start(out=bkva[:, :], in_=kva_d[:, :]), inc="gld", inc_n=16)
    sch.op("gpsimd", lambda e: e.dma_start(out=bkvb[:, :], in_=kvb_d[:, :]), inc="gld", inc_n=16)
    gld0 = sch.val("gld")
    sch.op("sync", lambda e: e.dma_start(
        out=ogl[:, :].rearrange("p (k t) -> p k t", t=TPC),
        in_=ogl_d.ap().rearrange("(k p) t -> p k t", p=128)), inc="ld", inc_n=16)
    sch.op("sync", lambda e: e.dma_start(out=id_sb[:, :], in_=ident_d[:, :]), inc="ld", inc_n=16)
    sch.op("sync", lambda e: e.dma_start(
        out=kva_sb[:, :].rearrange("p (k j) -> p k j", j=512),
        in_=bkva.ap().rearrange("(k p) j -> p k j", p=128)),
        waits=[("gld", gld0)], inc="ld", inc_n=16)
    sch.op("sync", lambda e: e.dma_start(
        out=kvb_sb[:, :].rearrange("p (k j) -> p k j", j=512),
        in_=bkvb.ap().rearrange("(k p) j -> p k j", p=128)), inc="ld", inc_n=16)
    ld0 = sch.val("ld")
    sch.op("vector", lambda e: e.memset(cst[:, 0:1], 1e-5), inc="ve")
    ve0 = sch.val("ve")

    marks = {}
    NT = TPC // 128
    for tt in range(NT):
        par = tt % 2
        m = marks.get(tt - 1, {})
        m2 = marks.get(tt - 2, {})
        sch.op("sync", lambda e, t=tt: e.dma_start(out=xsb[:, :], in_=x_d[t * 128:(t + 1) * 128, :]),
               waits=[("ve", m["ve_x1"])] if "ve_x1" in m else [], inc="ld", inc_n=16)
        sch.op("sync", lambda e, t=tt: e.dma_start(out=ysb[:, :], in_=y_dd[t * 128:(t + 1) * 128, :]),
               inc="ld", inc_n=16)
        ld_t = sch.val("ld")
        for oc in range(4):
            bank = oc % 2
            if tt == 0 and oc == 0:
                sch.op("sync", lambda e: e.dma_start(
                    out=wo_sb[:, :].rearrange("p (k j) -> p k j", j=2048),
                    in_=bWo.ap().rearrange("(k p) j -> p k j", p=128)),
                    waits=[("gld", gld0)], inc="ld", inc_n=16)
            w0 = [("ld", sch.val("ld"))] + wb(bank)
            for k in range(16):
                sch.op("tensor", lambda e, kk=k, t=tt, o=oc, b=bank: e.matmul(
                    ps[b][:, :], ogl[:, kk * TPC + t * 128:kk * TPC + (t + 1) * 128],
                    wo_sb[:, kk * 2048 + o * 512:kk * 2048 + (o + 1) * 512],
                    start=(kk == 0), stop=(kk == 15)),
                    waits=w0 if k == 0 else [], inc="te" if k == 15 else None)
            te_o = sch.val("te")
            sch.op("vector", lambda e, o=oc, b=bank: e.tensor_add(
                out=x1sb[:, o * 512:(o + 1) * 512], in0=xsb[:, o * 512:(o + 1) * 512],
                in1=ps[b][:, :]), waits=[("te", te_o), ("ld", ld_t)])
            sch.op("vector", lambda e, o=oc: e.tensor_add(
                out=x1sb[:, o * 512:(o + 1) * 512], in0=x1sb[:, o * 512:(o + 1) * 512],
                in1=ysb[:, o * 512:(o + 1) * 512]), inc="ve")
            bfree[bank] = [("ve", sch.val("ve"))]
        ve_x1 = sch.val("ve")
        sch.op("sync", lambda e, t=tt: e.dma_start(out=x1_o[t * 128:(t + 1) * 128, :], in_=x1sb[:, :]),
               waits=[("ve", ve_x1)], inc="stq", inc_n=16)
        sch.op("scalar", lambda e: e.activation(out=sq[:, :], in_=x1sb[:, :], func=AF.Square),
               waits=[("ve", ve_x1)] + ([("ve", m["ve_red"])] if "ve_red" in m else []), inc="se")
        se_sq = sch.val("se")
        sch.op("vector", lambda e: e.reduce_sum(out=st_[:, 0:1], in_=x1sb[:, :], axis=AX.X),
               waits=[("se", m["se_r"])] if "se_r" in m else [])
        sch.op("vector", lambda e: e.reduce_sum(out=st_[:, 2:3], in_=sq[:, :], axis=AX.X),
               waits=[("se", se_sq)])
        sch.op("vector", lambda e: e.tensor_scalar_mul(out=st_[:, 1:2], in0=st_[:, 0:1], scalar1=1.0 / D))
        sch.op("vector", lambda e: e.tensor_scalar_mul(out=st_[:, 3:4], in0=st_[:, 2:3], scalar1=1.0 / D))
        sch.op("vector", lambda e: e.tensor_mul(out=st_[:, 4:5], in0=st_[:, 1:2], in1=st_[:, 1:2]))
        sch.op("vector", lambda e: e.tensor_copy(out=cst[:, 1:2], in_=cst[:, 0:1]))
        sch.op("vector", lambda e: e.tensor_sub(out=st_[:, 5:6], in0=st_[:, 3:4], in1=st_[:, 4:5]))
        sch.op("vector", lambda e: e.tensor_copy(out=cst[:, 1:2], in_=cst[:, 0:1]), inc="ve")
        ve_red = sch.val("ve")
        sch.op("scalar", lambda e: e.activation(out=st_[:, 6:7], in_=st_[:, 5:6], func=AF.Ln,
                                                bias=cst[:, 0:1]), waits=[("ve", ve_red)])
        sch.op("scalar", lambda e: e.activation(out=st_[:, 7:8], in_=st_[:, 6:7], func=AF.Exp,
                                                scale=-0.5))
        sch.op("scalar", lambda e: e.activation(out=cst[:, 1:2], in_=cst[:, 0:1], func=AF.Copy),
               inc="se")
        se_r = sch.val("se")
        sch.op("vector", lambda e, t=tt: e.tensor_scalar(
            out=xn1b[:, :], in0=x1sb[:, :], scalar1=st_[:, 1:2], scalar2=st_[:, 7:8],
            op0=ALU.subtract, op1=ALU.mult),
            waits=[("se", se_r)] + ([("te", m["te_tr"])] if "te_tr" in m else []), inc="ve")
        ve_ln = sch.val("ve")
        for kk in range(16):
            wtr = [("ve", ve_ln)] + wb(4)
            sch.op("tensor", lambda e, k2=kk: e.transpose(
                ps[4].ap().bitcast(BF16)[:, 0:128], xn1b[:, k2 * 128:(k2 + 1) * 128], id_sb[:, :]),
                waits=wtr, inc="te")
            tev = sch.val("te")
            sch.op("vector", lambda e, k2=kk, t=tt: e.tensor_copy(
                out=xn1T_sb[:, k2 * TPC + t * 128:k2 * TPC + (t + 1) * 128],
                in_=ps[4].ap().bitcast(BF16)[:, 0:128]),
                waits=[("te", tev)], inc="ve")
            bfree[4] = [("ve", sch.val("ve"))]
        te_tr = sch.val("te")
        marks[tt] = dict(ve_x1=ve_x1, ve_red=ve_red, se_r=se_r, te_tr=te_tr)
    ve_all = sch.val("ve")
    sch.op("sync", lambda e: e.dma_start(
        out=xn1T_o.ap().rearrange("(k p) t -> p k t", p=128),
        in_=xn1T_sb[:, :].rearrange("p (k t) -> p k t", t=TPC)),
        waits=[("ve", ve_all)], inc="stq", inc_n=16)
    for oc in range(4):
        for hf in range(2):
            bank = 5 + (oc * 2 + hf) % 2
            for k in range(16):
                sch.op("tensor", lambda e, kk=k, o=oc, h=hf, b=bank: e.matmul(
                    ps[b][:, :], kva_sb[:, kk * 512 + o * 128:kk * 512 + (o + 1) * 128],
                    xn1T_sb[:, kk * TPC + h * 512:kk * TPC + h * 512 + 512],
                    start=(kk == 0), stop=(kk == 15)),
                    waits=([("ve", ve_all), ("ld", ld0)] + wb(bank)) if k == 0 else [],
                    inc="te" if k == 15 else None)
            tev = sch.val("te")
            sch.op("vector", lambda e, o=oc, h=hf, b=bank: e.tensor_copy(
                out=kvc_sb[:, o * TPC + h * 512:o * TPC + h * 512 + 512], in_=ps[b][:, :]),
                waits=[("te", tev)], inc="ve")
            bfree[bank] = [("ve", sch.val("ve"))]
    ve_kvc = sch.val("ve")
    for oc in range(4):
        for hf in range(2):
            bank = 5 + (oc * 2 + hf) % 2
            for k in range(4):
                sch.op("tensor", lambda e, kk=k, o=oc, h=hf, b=bank: e.matmul(
                    ps[b][:, :], kvb_sb[:, kk * 512 + o * 128:kk * 512 + (o + 1) * 128],
                    kvc_sb[:, kk * TPC + h * 512:kk * TPC + h * 512 + 512],
                    start=(kk == 0), stop=(kk == 3)),
                    waits=([("ve", ve_kvc)] + wb(bank)) if k == 0 else [],
                    inc="te" if k == 3 else None)
            tev = sch.val("te")
            sch.op("vector", lambda e, o=oc, h=hf, b=bank: e.tensor_copy(
                out=kvf_sb[:, o * TPC + h * 512:o * TPC + h * 512 + 512], in_=ps[b][:, :]),
                waits=[("te", tev)], inc="ve")
            bfree[bank] = [("ve", sch.val("ve"))]
    ve_kvf = sch.val("ve")
    sch.op("sync", lambda e: e.dma_start(
        out=kvf_o.ap().rearrange("(k p) t -> p k t", p=128),
        in_=kvf_sb[:, :].rearrange("p (k t) -> p k t", t=TPC)),
        waits=[("ve", ve_kvf)], inc="stq", inc_n=16)
    sch.op("sync", lambda e: e.dma_start(out=bkvb[0:1, 0:1], in_=bkva[0:1, 0:1]),
           waits=[("stq", sch.val("stq"))], inc="ld", inc_n=16)

    with ExitStack() as stack:
        sems = make_sems(nc, stack, ["ld", "gld", "te", "ve", "se", "stq"])
        sch.emit(nc, sems)
    return nc


def run_D(inputs, a_out, b_out, c_out):
    x_full = np.ascontiguousarray(np.asarray(inputs["L0_attn_x"], dtype=np.float32).reshape(TOK, D))
    og_fm = np.concatenate([np.asarray(c_out[c]["og"]) for c in range(NCORES)], axis=0)
    Wo = np.ascontiguousarray(np.asarray(inputs["L0_attn_Wo"], dtype=np.float32))
    kva = np.ascontiguousarray(np.asarray(inputs["L1_attn_kv_a_w"], dtype=np.float32))
    kvb = np.ascontiguousarray(np.asarray(inputs["L1_attn_kv_b_w"], dtype=np.float32))
    in_maps = []
    for c in range(NCORES):
        in_maps.append({
            "x": x_full[c * TPC:(c + 1) * TPC],
            "ogl": np.ascontiguousarray(og_fm[:, c * TPC:(c + 1) * TPC]),
            "Wo": Wo,
            "ymoe": np.ascontiguousarray(np.asarray(b_out[c]["y"])),
            "kva": kva, "kvb": kvb,
        })
    nc = build_D()
    res = run_bass_kernel_spmd(nc, in_maps, core_ids=list(range(NCORES)))
    return res.results


def build_E():
    nc = bass.Bass("TRN2", target_bir_lowering=False)
    qsrc = nc.dram_tensor("qsrc", [512, T], BF16, kind="ExternalInput")
    kvsl_d = nc.dram_tensor("kvsl", [128, T], BF16, kind="ExternalInput")
    cosD_d = nc.dram_tensor("cosD", [64, T], F32, kind="ExternalInput")
    sinD_d = nc.dram_tensor("sinD", [64, T], F32, kind="ExternalInput")
    heads_o = nc.dram_tensor("heads", [512, T], BF16, kind="ExternalOutput")
    ident_d = nc.inline_tensor(IDENT, name="identE")
    SC = 0.125

    kv_sb = nc.alloc_sbuf_tensor("kv_sbE", [128, T], BF16)
    ksw = nc.alloc_sbuf_tensor("kswE", [64, T], BF16)
    cosw = nc.alloc_sbuf_tensor("coswE", [64, T], F32)
    sinw = nc.alloc_sbuf_tensor("sinwE", [64, T], F32)
    u1 = nc.alloc_sbuf_tensor("u1E", [64, T], F32)
    s1 = nc.alloc_sbuf_tensor("s1E", [64, T], F32)
    kaug = nc.alloc_sbuf_tensor("kaugE", [65, T], BF16)
    qh_sb = nc.alloc_sbuf_tensor("qhE", [64, T], BF16)
    qsw = nc.alloc_sbuf_tensor("qswE", [64, T], BF16)
    qaug = nc.alloc_sbuf_tensor("qaugE", [65, T], BF16)
    sq2 = nc.alloc_sbuf_tensor("sq2E", [64, T], BF16)
    vaug = nc.alloc_sbuf_tensor("vaugE", [128, 32 * 66], BF16)
    id_sb = nc.alloc_sbuf_tensor("id_sbE", [128, 128], BF16)
    sgn = nc.alloc_sbuf_tensor("sgnE", [64, 1], F32)
    onesc = nc.alloc_sbuf_tensor("onescE", [64, 1], BF16)
    onesr = nc.alloc_sbuf_tensor("onesrE", [1, 64], F32)
    mkrow = nc.alloc_sbuf_tensor("mkrowE", [1, T], F32)
    mk2 = nc.alloc_sbuf_tensor("mk2E", [1, 4], F32)
    pt_sb = [nc.alloc_sbuf_tensor(f"ptE{i}", [128, 512], BF16) for i in range(2)]
    of_sb = nc.alloc_sbuf_tensor("ofE", [65, 512], F32)
    rl = nc.alloc_sbuf_tensor("rlE", [1, 512], F32)
    hd_sb = nc.alloc_sbuf_tensor("hdE", [64, 512], BF16)
    cc = nc.alloc_sbuf_tensor("ccE", [1, 4], F32)

    ps = [nc.alloc_psum_tensor(f"psE{i}", [128, 512], F32) for i in range(8)]

    sch = Sched()
    bfree = {i: [] for i in range(8)}

    def wb(i):
        return list(bfree[i])

    def gapv():
        sch.op("vector", lambda e: e.tensor_copy(out=cc[:, 2:3], in_=cc[:, 1:2]))

    sch.op("sync", lambda e: e.dma_start(out=kv_sb[:, :], in_=kvsl_d[:, :]), inc="ld", inc_n=16)
    sch.op("sync", lambda e: e.dma_start(out=ksw[0:32, :], in_=kvsl_d[32:64, :]), inc="ld", inc_n=16)
    sch.op("sync", lambda e: e.dma_start(out=ksw[32:64, :], in_=kvsl_d[0:32, :]), inc="ld", inc_n=16)
    sch.op("sync", lambda e: e.dma_start(out=cosw[:, :], in_=cosD_d[:, :]), inc="ld", inc_n=16)
    sch.op("sync", lambda e: e.dma_start(out=sinw[:, :], in_=sinD_d[:, :]), inc="ld", inc_n=16)
    sch.op("sync", lambda e: e.dma_start(out=id_sb[:, :], in_=ident_d[:, :]), inc="ld", inc_n=16)
    ld0 = sch.val("ld")
    sch.op("vector", lambda e: e.memset(sgn[0:32, :], -1.0))
    sch.op("vector", lambda e: e.memset(sgn[32:64, :], 1.0))
    sch.op("vector", lambda e: e.memset(onesc[:, :], 1.0))
    sch.op("vector", lambda e: e.memset(onesr[:, :], 1.0))
    sch.op("vector", lambda e: e.memset(vaug[:, :], 1.0))
    sch.op("vector", lambda e: e.memset(kaug[64:65, :].bitcast(F32) if False else kaug[64:65, :], 1.0),
           inc="ve")
    ve0 = sch.val("ve")
    sch.op("vector", lambda e: e.tensor_mul(out=u1[:, :], in0=kv_sb[0:64, :], in1=cosw[:, :]),
           waits=[("ld", ld0)])
    sch.op("vector", lambda e: e.tensor_mul(out=s1[:, :], in0=ksw[:, :], in1=sinw[:, :]))
    sch.op("vector", lambda e: e.tensor_scalar(out=s1[:, :], in0=s1[:, :], scalar1=sgn[:, 0:1],
                                               scalar2=None, op0=ALU.mult))
    gapv()
    sch.op("vector", lambda e: e.tensor_add(out=u1[:, :], in0=u1[:, :], in1=s1[:, :]))
    gapv()
    sch.op("vector", lambda e: e.tensor_scalar(out=kaug[0:64, :], in0=u1[:, :], scalar1=SC,
                                               scalar2=None, op0=ALU.mult), inc="ve")
    ve_k = sch.val("ve")
    sch.op("scalar", lambda e: e.activation(out=sq2[:, :], in_=kaug[0:64, :], func=AF.Square),
           waits=[("ve", ve_k)], inc="se")
    se_k2 = sch.val("se")
    for kb in range(8):
        sch.op("tensor", lambda e, k2=kb: e.matmul(
            ps[4][0:1, :], onesc[:, :], sq2[:, k2 * 512:(k2 + 1) * 512],
            start=True, stop=True),
            waits=([("se", se_k2), ("ve", ve0)] + wb(4)) if kb == 0 else [],
            inc="te" if kb == 7 else None)
        if kb < 7:
            pass
        tev = sch.val("te") + (0 if kb == 7 else 0)
    sch.ops["tensor"] = sch.ops["tensor"]
    te_k2 = sch.val("te")
    sch.op("vector", lambda e: e.tensor_copy(out=mkrow[0:1, :].rearrange("o (kb j) -> o kb j", j=512)
                                             if False else mkrow[0:1, 0:512], in_=ps[4][0:1, :]),
           waits=[("te", te_k2)], inc="ve")
    with ExitStack() as stack:
        pass
    raise NotImplementedError("placeholder")

def build_E():
    nc = bass.Bass("TRN2", target_bir_lowering=False)
    qsrc = nc.dram_tensor("qsrc", [512, T], BF16, kind="ExternalInput")
    kvsl_d = nc.dram_tensor("kvsl", [128, T], BF16, kind="ExternalInput")
    cosD_d = nc.dram_tensor("cosD", [64, T], F32, kind="ExternalInput")
    sinD_d = nc.dram_tensor("sinD", [64, T], F32, kind="ExternalInput")
    heads_o = nc.dram_tensor("heads", [512, T], BF16, kind="ExternalOutput")
    ident_d = nc.inline_tensor(IDENT, name="identE")
    SC = 0.125

    kv_sb = nc.alloc_sbuf_tensor("kv_sbE", [128, T], BF16)
    ksw = nc.alloc_sbuf_tensor("kswE", [64, T], BF16)
    cosw = nc.alloc_sbuf_tensor("coswE", [64, T], F32)
    sinw = nc.alloc_sbuf_tensor("sinwE", [64, T], F32)
    u1 = nc.alloc_sbuf_tensor("u1E", [64, T], F32)
    s1 = nc.alloc_sbuf_tensor("s1E", [64, T], F32)
    kaug = nc.alloc_sbuf_tensor("kaugE", [65, T], BF16)
    qh_sb = nc.alloc_sbuf_tensor("qhE", [64, T], BF16)
    qsw = nc.alloc_sbuf_tensor("qswE", [64, T], BF16)
    qaug = nc.alloc_sbuf_tensor("qaugE", [65, T], BF16)
    sq2 = nc.alloc_sbuf_tensor("sq2E", [64, T], BF16)
    vaug = nc.alloc_sbuf_tensor("vaugE", [128, 32 * 66], BF16)
    id_sb = nc.alloc_sbuf_tensor("id_sbE", [128, 128], BF16)
    sgn = nc.alloc_sbuf_tensor("sgnE", [64, 1], F32)
    onesc = nc.alloc_sbuf_tensor("onescE", [64, 1], BF16)
    onesr = nc.alloc_sbuf_tensor("onesrE", [1, 64], F32)
    mkrow = nc.alloc_sbuf_tensor("mkrowE", [1, T], F32)
    mk2 = nc.alloc_sbuf_tensor("mk2E", [1, 4], F32)
    pt_sb = [nc.alloc_sbuf_tensor(f"ptE{i}", [128, 512], BF16) for i in range(2)]
    of_sb = nc.alloc_sbuf_tensor("ofE", [65, 512], F32)
    rl = nc.alloc_sbuf_tensor("rlE", [1, 512], F32)
    hd_sb = [nc.alloc_sbuf_tensor(f"hdE{i}", [64, 512], BF16) for i in range(2)]
    cc = nc.alloc_sbuf_tensor("ccE", [1, 4], F32)

    ps = [nc.alloc_psum_tensor(f"psE{i}", [128, 512], F32) for i in range(8)]

    sch = Sched()
    bfree = {i: [] for i in range(8)}

    def wb(i):
        return list(bfree[i])

    def gapv():
        sch.op("vector", lambda e: e.tensor_copy(out=cc[:, 2:3], in_=cc[:, 1:2]))

    sch.op("sync", lambda e: e.dma_start(out=kv_sb[:, :], in_=kvsl_d[:, :]), inc="ld", inc_n=16)
    sch.op("sync", lambda e: e.dma_start(out=ksw[0:32, :], in_=kvsl_d[32:64, :]), inc="ld", inc_n=16)
    sch.op("sync", lambda e: e.dma_start(out=ksw[32:64, :], in_=kvsl_d[0:32, :]), inc="ld", inc_n=16)
    sch.op("sync", lambda e: e.dma_start(out=cosw[:, :], in_=cosD_d[:, :]), inc="ld", inc_n=16)
    sch.op("sync", lambda e: e.dma_start(out=sinw[:, :], in_=sinD_d[:, :]), inc="ld", inc_n=16)
    sch.op("sync", lambda e: e.dma_start(out=id_sb[:, :], in_=ident_d[:, :]), inc="ld", inc_n=16)
    ld0 = sch.val("ld")
    sch.op("vector", lambda e: e.memset(sgn[0:32, :], -1.0))
    sch.op("vector", lambda e: e.memset(sgn[32:64, :], 1.0))
    sch.op("vector", lambda e: e.memset(onesc[:, :], 1.0))
    sch.op("vector", lambda e: e.memset(onesr[:, :], 1.0))
    sch.op("vector", lambda e: e.memset(vaug[:, :], 1.0))
    sch.op("vector", lambda e: e.memset(kaug[64:65, :], 1.0), inc="ve")
    ve0 = sch.val("ve")
    sch.op("vector", lambda e: e.tensor_mul(out=u1[:, :], in0=kv_sb[0:64, :], in1=cosw[:, :]),
           waits=[("ld", ld0)])
    sch.op("vector", lambda e: e.tensor_mul(out=s1[:, :], in0=ksw[:, :], in1=sinw[:, :]))
    sch.op("vector", lambda e: e.tensor_scalar(out=s1[:, :], in0=s1[:, :], scalar1=sgn[:, 0:1],
                                               scalar2=None, op0=ALU.mult))
    gapv()
    sch.op("vector", lambda e: e.tensor_add(out=u1[:, :], in0=u1[:, :], in1=s1[:, :]))
    gapv()
    sch.op("vector", lambda e: e.tensor_scalar(out=kaug[0:64, :], in0=u1[:, :], scalar1=SC,
                                               scalar2=None, op0=ALU.mult), inc="ve")
    ve_k = sch.val("ve")
    sch.op("scalar", lambda e: e.activation(out=sq2[:, :], in_=kaug[0:64, :], func=AF.Square),
           waits=[("ve", ve_k)], inc="se")
    se_k2 = sch.val("se")
    for kb in range(8):
        bank = 4 + kb % 2
        sch.op("tensor", lambda e, k2=kb, b=bank: e.matmul(
            ps[b][0:1, :], onesc[:, :], sq2[:, k2 * 512:(k2 + 1) * 512], start=True, stop=True),
            waits=[("se", se_k2)] + wb(bank), inc="te")
        tev = sch.val("te")
        sch.op("vector", lambda e, k2=kb, b=bank: e.tensor_copy(
            out=mkrow[0:1, k2 * 512:(k2 + 1) * 512], in_=ps[b][0:1, :]),
            waits=[("te", tev)], inc="ve")
        bfree[bank] = [("ve", sch.val("ve"))]
    sch.op("vector", lambda e: e.reduce_max(out=mk2[0:1, 0:1], in_=mkrow[0:1, :], axis=AX.X))
    gapv()
    sch.op("vector", lambda e: e.tensor_scalar(out=mk2[0:1, 1:2], in0=mk2[0:1, 0:1],
                                               scalar1=-0.5, scalar2=None, op0=ALU.mult))
    gapv()
    sch.op("vector", lambda e: e.tensor_copy(out=cc[:, 2:3], in_=cc[:, 1:2]), inc="ve")
    ve_mk = sch.val("ve")
    for kt in range(32):
        sch.op("tensor", lambda e, k2=kt: e.transpose(
            ps[6].ap().bitcast(BF16)[:, 0:64], kv_sb[64:128, k2 * 128:(k2 + 1) * 128],
            id_sb[64:128, 64:128]),
            waits=[("ld", ld0)] + wb(6), inc="te")
        tev = sch.val("te")
        sch.op("vector", lambda e, k2=kt: e.tensor_copy(
            out=vaug[:, k2 * 66:k2 * 66 + 64], in_=ps[6].ap().bitcast(BF16)[:, 0:64]),
            waits=[("te", tev), ("ve", ve0)], inc="ve")
        bfree[6] = [("ve", sch.val("ve"))]
    ve_v = sch.val("ve")

    hmark = {}
    for lh in range(8):
        hm = hmark.get(lh - 1, {})
        wq = [("ve", hm["ve_hd"])] if hm else []
        sch.op("sync", lambda e, l=lh: e.dma_start(out=qh_sb[:, :], in_=qsrc[l * 64:(l + 1) * 64, :]),
               waits=([("ve", hm["ve_q"])] if hm else []), inc="ld", inc_n=16)
        sch.op("sync", lambda e, l=lh: e.dma_start(out=qsw[0:32, :], in_=qsrc[l * 64 + 32:(l + 1) * 64, :]),
               inc="ld", inc_n=16)
        sch.op("sync", lambda e, l=lh: e.dma_start(out=qsw[32:64, :], in_=qsrc[l * 64:l * 64 + 32, :]),
               inc="ld", inc_n=16)
        ld_q = sch.val("ld")
        sch.op("vector", lambda e: e.tensor_mul(out=u1[:, :], in0=qh_sb[:, :], in1=cosw[:, :]),
               waits=[("ld", ld_q), ("ve", ve_mk)] + ([("se", hm["se_q"])] if hm else []))
        sch.op("vector", lambda e: e.tensor_mul(out=s1[:, :], in0=qsw[:, :], in1=sinw[:, :]))
        sch.op("vector", lambda e: e.tensor_scalar(out=s1[:, :], in0=s1[:, :], scalar1=sgn[:, 0:1],
                                                   scalar2=None, op0=ALU.mult))
        gapv()
        sch.op("vector", lambda e: e.tensor_add(out=qaug[0:64, :], in0=u1[:, :], in1=s1[:, :]),
               inc="ve")
        ve_q = sch.val("ve")
        sch.op("scalar", lambda e: e.activation(out=sq2[:, :], in_=qaug[0:64, :], func=AF.Square),
               waits=[("ve", ve_q)], inc="se")
        se_q = sch.val("se")
        for qb in range(8):
            bank = 4 + qb % 2
            sch.op("tensor", lambda e, q2=qb, b=bank: e.matmul(
                ps[b][0:1, :], onesc[:, :], sq2[:, q2 * 512:(q2 + 1) * 512], start=True, stop=True),
                waits=[("se", se_q)] + wb(bank), inc="te")
            tev = sch.val("te")
            sch.op("vector", lambda e, q2=qb, b=bank: e.tensor_scalar(
                out=qaug[64:65, q2 * 512:(q2 + 1) * 512], in0=ps[b][0:1, :],
                scalar1=-0.5, scalar2=mk2[0:1, 1:2], op0=ALU.mult, op1=ALU.add),
                waits=[("te", tev)], inc="ve")
            bfree[bank] = [("ve", sch.val("ve"))]
        ve_mb = sch.val("ve")
        for qb in range(8):
            nkt = 4 * qb + 4
            for kt in range(nkt):
                stb = kt % 2
                w_st = [("ve", ve_mb)] + wb(stb)
                sch.op("tensor", lambda e, k2=kt, q2=qb, b=stb: e.matmul(
                    ps[b][:, :], kaug[:, k2 * 128:(k2 + 1) * 128],
                    qaug[:, q2 * 512:(q2 + 1) * 512], start=True, stop=True),
                    waits=w_st, inc="te")
                te_st = sch.val("te")
                sch.op("scalar", lambda e, b=stb, p=stb: e.activation(
                    out=pt_sb[p][:, :], in_=ps[b][:, :], func=AF.Exp),
                    waits=[("te", te_st)] + ([("te", sch.val("te2v"))] if False else []) +
                          ([("gp", sch.val("gp"))] if sch.val("gp") > 0 and kt >= 2 else []),
                    inc="se")
                se_e = sch.val("se")
                bfree[stb] = [("se", se_e)]
                diag = (kt >= 4 * qb)
                if diag:
                    sch.op("gpsimd", lambda e, k2=kt, q2=qb, p=stb: e.affine_select(
                        out=pt_sb[p][:, :], in_=pt_sb[p][:, :], pattern=[[1, 512]],
                        compare_op=ALU.is_ge, fill=0.0,
                        base=q2 * 512 - k2 * 128, channel_multiplier=-1),
                        waits=[("se", se_e)], inc="gp")
                wpv = [("se", se_e)] if not diag else [("gp", sch.val("gp"))]
                if kt == 0:
                    wpv = wpv + wb(2)
                sch.op("tensor", lambda e, k2=kt, p=stb, q2=qb, n=nkt: e.matmul(
                    ps[2][0:65, :], vaug[:, k2 * 66:k2 * 66 + 65], pt_sb[p][:, :],
                    start=(k2 == 0), stop=(k2 == n - 1)),
                    waits=wpv, inc="te")
            te_pv = sch.val("te")
            sch.op("vector", lambda e: e.tensor_copy(out=of_sb[:, :], in_=ps[2][0:65, :]),
                   waits=[("te", te_pv)], inc="ve")
            bfree[2] = [("ve", sch.val("ve"))]
            sch.op("vector", lambda e: e.reciprocal(out=rl[0:1, :], in_=of_sb[64:65, :]), inc="ve")
            ve_rl = sch.val("ve")
            sch.op("tensor", lambda e: e.matmul(ps[3][0:64, :], onesr[:, :], rl[0:1, :],
                                                start=True, stop=True),
                   waits=[("ve", ve_rl)] + wb(3), inc="te")
            te_bc = sch.val("te")
            par = qb % 2
            sch.op("vector", lambda e, p=par: e.tensor_mul(out=hd_sb[p][:, :], in0=of_sb[0:64, :],
                                                           in1=ps[3][0:64, :]),
                   waits=[("te", te_bc), ("stq", max(0, sch.val("stq") - 16))], inc="ve")
            ve_hd = sch.val("ve")
            bfree[3] = [("ve", ve_hd)]
            sch.op("sync", lambda e, l=lh, q2=qb, p=par: e.dma_start(
                out=heads_o[l * 64:(l + 1) * 64, q2 * 512:(q2 + 1) * 512], in_=hd_sb[p][:, :]),
                waits=[("ve", ve_hd)], inc="stq", inc_n=16)
        hmark[lh] = dict(ve_q=ve_q, se_q=se_q, ve_hd=sch.val("ve"))
    sch.op("sync", lambda e: e.dma_start(out=mk2[0:1, 3:4].bitcast(F32) if False else kv_sb[0:1, 0:1],
                                         in_=kv_sb[0:1, 1:2]),
           waits=[("stq", sch.val("stq"))], inc="ld", inc_n=16)

    with ExitStack() as stack:
        sems = make_sems(nc, stack, ["ld", "te", "ve", "se", "gp", "stq"])
        sch.emit(nc, sems)
    return nc


ROPE_COS = None
ROPE_SIN = None


def _rope_tables():
    global ROPE_COS, ROPE_SIN
    if ROPE_COS is None:
        inv = 1.0 / (10000.0 ** (np.arange(0, 64, 2, dtype=np.float64) / 64))
        pos = np.arange(T, dtype=np.float64)
        ang = pos[:, None] * inv[None, :]
        c = np.cos(ang).astype(np.float32).T
        s_ = np.sin(ang).astype(np.float32).T
        ROPE_COS = np.ascontiguousarray(np.concatenate([c, c], axis=0))
        ROPE_SIN = np.ascontiguousarray(np.concatenate([s_, s_], axis=0))
    return ROPE_COS, ROPE_SIN


def run_E(inputs, d_out):
    xn1T_full = np.concatenate([np.asarray(d_out[c]["xn1T"]) for c in range(NCORES)], axis=1)
    kvf_full = np.concatenate([np.asarray(d_out[c]["kvf"]) for c in range(NCORES)], axis=1)
    cosD, sinD = _rope_tables()
    in_maps = []
    for c in range(NCORES):
        b = c // 4
        g = c % 4
        in_maps.append({
            "qsrc": np.ascontiguousarray(xn1T_full[512 * g:512 * (g + 1), b * T:(b + 1) * T]),
            "kvsl": np.ascontiguousarray(kvf_full[128 * g:128 * (g + 1), b * T:(b + 1) * T]),
            "cosD": cosD, "sinD": sinD,
        })
    nc = build_E()
    res = run_bass_kernel_spmd(nc, in_maps, core_ids=list(range(NCORES)))
    return res.results



def build_F(debug=False):
    nc = bass.Bass("TRN2", target_bir_lowering=False)
    x1_d = nc.dram_tensor("x1", [TPC, D], F32, kind="ExternalInput")
    hl_d = nc.dram_tensor("headsl", [D, TPC], BF16, kind="ExternalInput")
    ow_d = nc.dram_tensor("ow", [D, D], F32, kind="ExternalInput")
    wguT_d = nc.dram_tensor("wguT_f", [D, 11264], BF16, kind="ExternalInput")
    wdnT_d = nc.dram_tensor("wdnT_f", [5632, D], BF16, kind="ExternalInput")
    xn1T_d = nc.dram_tensor("xn1T_f", [D, TPC], BF16, kind="ExternalInput")
    out_o = nc.dram_tensor("out", [TPC, D], F32, kind="ExternalOutput")

    bow = nc.dram_tensor("bow", [D, D], BF16, kind="Internal")
    att1_d = nc.dram_tensor("att1", [TPC, D], F32, kind="ExternalOutput" if debug else "Internal")
    y1_dbg = nc.dram_tensor("y1dbg", [128, 8192], F32, kind="ExternalOutput" if debug else "Internal")
    h_dbg = nc.dram_tensor("hdbg", [128, 2048], BF16, kind="ExternalOutput" if debug else "Internal")

    hl = nc.alloc_sbuf_tensor("hlF", [128, 16 * TPC], BF16)
    xn1T = nc.alloc_sbuf_tensor("xn1TF", [128, 16 * TPC], BF16)
    wow = nc.alloc_sbuf_tensor("wowF", [128, 16 * 512], BF16)
    hT = nc.alloc_sbuf_tensor("hTF", [128, 44 * 512], BF16)
    y1 = nc.alloc_sbuf_tensor("y1F", [128, 4 * 2048], F32)
    gst = [nc.alloc_sbuf_tensor(f"gstF{i}", [128, 16 * 128], BF16) for i in range(2)]
    ust = [nc.alloc_sbuf_tensor(f"ustF{i}", [128, 16 * 128], BF16) for i in range(2)]
    dnt = [nc.alloc_sbuf_tensor(f"dntF{i}", [128, 512], BF16) for i in range(2)]
    sgf = nc.alloc_sbuf_tensor("sgfF", [128, 512], BF16)
    xt = nc.alloc_sbuf_tensor("xtF", [128, D], F32)
    at = nc.alloc_sbuf_tensor("atF", [128, D], F32)
    ev = nc.alloc_sbuf_tensor("evF", [128, 512], F32)
    kst = nc.alloc_sbuf_tensor("kstF", [128, 16], F32)
    cc = nc.alloc_sbuf_tensor("ccF", [128, 8], F32)
    yout = nc.alloc_sbuf_tensor("youtF", [128, D], F32)
    sqd = nc.alloc_sbuf_tensor("sqdF", [128, D], BF16)

    ps = [nc.alloc_psum_tensor(f"psF{i}", [128, 512], F32) for i in range(8)]

    sch = Sched()
    bfree = {i: [] for i in range(8)}

    def wb(i):
        return list(bfree[i])

    def gapv():
        sch.op("vector", lambda e: e.tensor_copy(out=cc[:, 7:8], in_=cc[:, 6:7]))

    sch.op("gpsimd", lambda e: e.dma_start(out=bow[:, :], in_=ow_d[:, :]), inc="gld", inc_n=16)
    gld0 = sch.val("gld")
    sch.op("sync", lambda e: e.dma_start(
        out=hl[:, :].rearrange("p (k t) -> p k t", t=TPC),
        in_=hl_d.ap().rearrange("(k p) t -> p k t", p=128)), inc="ld", inc_n=16)
    sch.op("sync", lambda e: e.dma_start(
        out=xn1T[:, :].rearrange("p (k t) -> p k t", t=TPC),
        in_=xn1T_d.ap().rearrange("(k p) t -> p k t", p=128)), inc="ld", inc_n=16)
    ld0 = sch.val("ld")
    sch.op("vector", lambda e: e.memset(cc[:, 0:1], 1e-6))
    sch.op("vector", lambda e: e.memset(cc[:, 6:7], 0.5), inc="ve")
    ve0 = sch.val("ve")

    for oc in range(4):
        sch.op("sync", lambda e, o=oc: e.dma_start(
            out=wow[:, :].rearrange("p (k j) -> p k j", j=512),
            in_=bow.ap()[:, o * 512:(o + 1) * 512].rearrange("(k p) j -> p k j", p=128)),
            waits=[("gld", gld0)] + ([("te", sch.val("te"))] if oc > 0 else []), inc="ld", inc_n=16)
        ld_w = sch.val("ld")
        for tt in range(8):
            bank = tt % 2
            for k in range(16):
                sch.op("tensor", lambda e, kk=k, t=tt, b=bank: e.matmul(
                    ps[b][:, :], hl[:, kk * TPC + t * 128:kk * TPC + (t + 1) * 128],
                    wow[:, kk * 512:(kk + 1) * 512], start=(kk == 0), stop=(kk == 15)),
                    waits=([("ld", max(ld_w, ld0))] + wb(bank)) if k == 0 else [],
                    inc="te" if k == 15 else None)
            tev = sch.val("te")
            sch.op("vector", lambda e, b=bank: e.tensor_copy(out=ev[:, :], in_=ps[b][:, :]),
                   waits=[("te", tev), ("stq", max(0, sch.val("stq") - 16))], inc="ve")
            vev = sch.val("ve")
            bfree[bank] = [("ve", vev)]
            sch.op("sync", lambda e, t=tt, o=oc: e.dma_start(
                out=att1_d[t * 128:(t + 1) * 128, o * 512:(o + 1) * 512], in_=ev[:, :]),
                waits=[("ve", vev)], inc="stq", inc_n=16)
    st_att = sch.val("stq")

    for tb2 in range(2):
        for oc in range(44):
            par = oc % 2
            wld = [("te", sch.val("te"))] if (tb2 > 0 or oc >= 2) else []
            sch.op("sync", lambda e, o=oc, p=par: e.dma_start(
                out=gst[p][:, :].rearrange("p2 (k j) -> p2 k j", j=128),
                in_=wguT_d.ap()[:, o * 128:(o + 1) * 128].rearrange("(k p2) j -> p2 k j", p2=128)),
                waits=wld, inc="ld", inc_n=16)
            sch.op("sync", lambda e, o=oc, p=par: e.dma_start(
                out=ust[p][:, :].rearrange("p2 (k j) -> p2 k j", j=128),
                in_=wguT_d.ap()[:, (44 + o) * 128:(45 + o) * 128].rearrange("(k p2) j -> p2 k j", p2=128)),
                inc="ld", inc_n=16)
            ld_s = sch.val("ld")
            bg = oc % 2
            bu = 2 + oc % 2
            for k in range(16):
                sch.op("tensor", lambda e, kk=k, p=par, b=bg, t2=tb2: e.matmul(
                    ps[b][:, 0:128], gst[p][:, kk * 128:(kk + 1) * 128],
                    xn1T[:, kk * TPC + t2 * 512:kk * TPC + t2 * 512 + 512][:, 0:512],
                    start=(kk == 0), stop=(kk == 15)) if False else e.matmul(
                    ps[b][:, :], gst[p][:, kk * 128:(kk + 1) * 128],
                    xn1T[:, kk * TPC + t2 * 512:kk * TPC + (t2) * 512 + 512],
                    start=(kk == 0), stop=(kk == 15)),
                    waits=([("ld", ld_s)] + wb(bg)) if k == 0 else [])
            for k in range(16):
                sch.op("tensor", lambda e, kk=k, p=par, b=bu, t2=tb2: e.matmul(
                    ps[b][:, :], ust[p][:, kk * 128:(kk + 1) * 128],
                    xn1T[:, kk * TPC + t2 * 512:kk * TPC + t2 * 512 + 512],
                    start=(kk == 0), stop=(kk == 15)),
                    waits=wb(bu) if k == 0 else [], inc="te" if k == 15 else None)
            tev = sch.val("te")
            sch.op("scalar", lambda e, b=bg: e.activation(out=sgf[:, :], in_=ps[b][:, :],
                                                          func=AF.Silu),
                   waits=[("te", tev)] + ([("ve", sch.val("ve"))] if oc >= 1 or tb2 > 0 else []),
                   inc="se")
            sev = sch.val("se")
            bfree[bg] = [("se", sev)]
            sch.op("vector", lambda e, o=oc, b=bu: e.tensor_mul(
                out=hT[:, o * 512:(o + 1) * 512], in0=sgf[:, :], in1=ps[b][:, :]),
                waits=[("se", sev)], inc="ve")
            bfree[bu] = [("ve", sch.val("ve"))]
        ve_h = sch.val("ve")
        for oc2 in range(4):
            te_i = {}
            for i in range(44):
                par = i % 2
                wldd = [("te", te_i[i - 2])] if i >= 2 else (
                    [("te", sch.val("te"))] if (oc2 > 0 or tb2 > 0) else [])
                sch.op("sync", lambda e, ii=i, o=oc2, p=par: e.dma_start(
                    out=dnt[p][:, :], in_=wdnT_d.ap()[ii * 128:(ii + 1) * 128, o * 512:(o + 1) * 512]),
                    waits=wldd, inc="ld", inc_n=16)
                ld_d = sch.val("ld")
                for tt in range(4):
                    bank = 4 + tt
                    sch.op("tensor", lambda e, ii=i, t=tt, p=par, b=bank, o=oc2, t2=tb2: e.matmul(
                        ps[b][:, :], hT[:, ii * 512 + t * 128:ii * 512 + (t + 1) * 128],
                        dnt[p][:, :], start=(ii == 0), stop=(ii == 43)),
                        waits=([("ld", ld_d), ("ve", ve_h)] + (wb(bank) if i == 0 else []))
                        if (i == 0 or tt == 0) else [],
                        inc="te" if tt == 3 else None)
                te_i[i] = sch.val("te")
            te_d = sch.val("te")
            for tt in range(4):
                bank = 4 + tt
                sch.op("vector", lambda e, t=tt, b=bank, o=oc2: e.tensor_copy(
                    out=y1[:, t * 2048 + o * 512:t * 2048 + (o + 1) * 512], in_=ps[b][:, :]),
                    waits=[("te", te_d)], inc="ve")
                bfree[bank] = [("ve", sch.val("ve"))]
        ve_y = sch.val("ve")
        if debug and tb2 == 0:
            sch.op("sync", lambda e: e.dma_start(out=y1_dbg[:, :], in_=y1[:, :]),
                   waits=[("ve", ve_y)], inc="stq", inc_n=16)
            sch.op("sync", lambda e: e.dma_start(out=h_dbg[:, :], in_=hT[:, 0:2048]),
                   waits=[("ve", ve_y)], inc="stq", inc_n=16)
        for tt in range(4):
            gt = tb2 * 4 + tt
            sch.op("sync", lambda e, g=gt: e.dma_start(out=xt[:, :], in_=x1_d[g * 128:(g + 1) * 128, :]),
                   waits=[("ve", sch.val("ve")), ("stq", st_att)], inc="ld", inc_n=16)
            sch.op("sync", lambda e, g=gt: e.dma_start(out=at[:, :], in_=att1_d[g * 128:(g + 1) * 128, :]),
                   inc="ld", inc_n=16)
            ld_x = sch.val("ld")
            sch.op("scalar", lambda e, t=tt: e.activation(
                out=sgf[:, :].bitcast(F32)[:, 0:256] if False else kst[:, 8:9],
                in_=y1[:, t * 2048:(t + 1) * 2048], func=AF.Square, accum_out=kst[:, 0:1]),
                waits=[("ve", ve_y)], inc="se") if False else None
            sch.op("scalar", lambda e, t=tt: e.activation(
                out=sqd[:, :], in_=y1[:, t * 2048:(t + 1) * 2048],
                func=AF.Square, accum_out=kst[:, 0:1]),
                waits=[("ve", ve_y), ("k1", sch.val("k1"))], inc="se")
            se_sq = sch.val("se")
            sch.op("vector", lambda e: e.tensor_scalar(out=kst[:, 1:2], in0=kst[:, 0:1],
                                                       scalar1=1.0 / D, scalar2=1e-6,
                                                       op0=ALU.mult, op1=ALU.add),
                   waits=[("se", se_sq)])
            gapv()
            sch.op("vector", lambda e: e.tensor_copy(out=cc[:, 7:8], in_=cc[:, 6:7]), inc="k1")
            newton_rsqrt(sch, "vector", kst, 1, 2, 4, cc)
            sch.op("vector", lambda e, t=tt: e.tensor_scalar(
                out=yout[:, :], in0=y1[:, t * 2048:(t + 1) * 2048], scalar1=kst[:, 2:3],
                scalar2=None, op0=ALU.mult), waits=[("ld", ld_x)])
            gapv()
            sch.op("vector", lambda e: e.tensor_add(out=yout[:, :], in0=yout[:, :], in1=xt[:, :]))
            gapv()
            sch.op("vector", lambda e: e.tensor_add(out=yout[:, :], in0=yout[:, :], in1=at[:, :]),
                   inc="ve")
            ve_o = sch.val("ve")
            sch.op("sync", lambda e, g=gt: e.dma_start(out=out_o[g * 128:(g + 1) * 128, :],
                                                       in_=yout[:, :]),
                   waits=[("ve", ve_o)], inc="stq", inc_n=16)
    sch.op("sync", lambda e: e.dma_start(out=bow[0:1, 0:1], in_=bow[0:1, 1:2]),
           waits=[("stq", sch.val("stq"))], inc="ld", inc_n=16)

    with ExitStack() as stack:
        sems = make_sems(nc, stack, ["ld", "gld", "te", "ve", "se", "stq", "k1"])
        sch.emit(nc, sems)
    return nc


def run_F(inputs, d_out, e_out, a_out, debug=False):
    x1_full = np.concatenate([np.asarray(d_out[c]["x1"]) for c in range(NCORES)], axis=0)
    heads_b = []
    for b in range(2):
        heads_b.append(np.concatenate([np.asarray(e_out[4 * b + g]["heads"]) for g in range(4)], axis=0))
    ow = np.ascontiguousarray(np.asarray(inputs["L1_attn_o_w"], dtype=np.float32))
    wguT = np.ascontiguousarray(np.concatenate([np.asarray(a_out[c]["wguT"]) for c in range(NCORES)], axis=1))
    wdnT = np.ascontiguousarray(np.concatenate([np.asarray(a_out[c]["wdnT"]) for c in range(NCORES)], axis=1))
    in_maps = []
    for c in range(NCORES):
        b = c // 4
        tok0 = (c * TPC) % T
        in_maps.append({
            "x1": np.ascontiguousarray(x1_full[c * TPC:(c + 1) * TPC]),
            "headsl": np.ascontiguousarray(heads_b[b][:, tok0:tok0 + TPC]),
            "ow": ow, "wguT_f": wguT, "wdnT_f": wdnT,
            "xn1T_f": np.asarray(d_out[c]["xn1T"]),
        })
    nc = build_F(debug=debug)
    res = run_bass_kernel_spmd(nc, in_maps, core_ids=list(range(NCORES)))
    return res.results
